# version 01
# baseline (speedup 1.0000x reference)
import numpy as np
import ml_dtypes
import concourse.bass as bass
import concourse.mybir as mybir
from concourse.bass_utils import run_bass_kernel_spmd
from contextlib import ExitStack

F32 = mybir.dt.float32
BF16 = mybir.dt.bfloat16
AF = mybir.ActivationFunctionType
ALU = mybir.AluOpType
AX = mybir.AxisListType

L = 4096
D = 2048
DWN = 512
UP = 1024
NH = 16
RD = 32
VD = 64
IF = 5632
OGU = 11264
SCALE = float(1.0 / (16 ** 0.5 + 32 ** 0.5))
MAGIC = 12582912.0
NST = 8
NKV = 32
NOG1 = 22
NOC2 = 16
NIC2 = 44

DEBUG_PHASE = 99
DEBUG_P4STAGE = "F"
DEBUG_SCRATCH = False


class C:
    __slots__ = ("sem", "n")
    def __init__(self, sem):
        self.sem = sem
        self.n = 0
    def inc(self, instr, k=1):
        instr.then_inc(self.sem, k)
        self.n += k
        return self.n
    def wait(self, eng, val=None):
        v = self.n if val is None else val
        if v > 0:
            eng.wait_ge(self.sem, v)


class SelfW:
    __slots__ = ("sem", "eng", "n")
    def __init__(self, sem, eng):
        self.sem = sem
        self.eng = eng
        self.n = 0
    def __call__(self, instr):
        self.n += 1
        instr.then_inc(self.sem, 1)
        self.eng.wait_ge(self.sem, self.n)


def build_nc():
    nc = bass.Bass("TRN2", target_bir_lowering=False)
    V, S, T, G, SY = nc.vector, nc.scalar, nc.tensor, nc.gpsimd, nc.sync
    skind = "ExternalOutput" if DEBUG_SCRATCH else "Internal"

    x_d = nc.dram_tensor("x", [L, D], F32, kind="ExternalInput").ap()
    w_dkv = nc.dram_tensor("wdkv", [D, DWN], F32, kind="ExternalInput").ap()
    w_uk = nc.dram_tensor("wuk", [DWN, UP], F32, kind="ExternalInput").ap()
    w_uv = nc.dram_tensor("wuv", [DWN, UP], F32, kind="ExternalInput").ap()
    w_dq = nc.dram_tensor("wdq", [D, DWN], F32, kind="ExternalInput").ap()
    w_uq = nc.dram_tensor("wuq", [DWN, UP], F32, kind="ExternalInput").ap()
    w_qr = nc.dram_tensor("wqr", [DWN, DWN], F32, kind="ExternalInput").ap()
    w_kr = nc.dram_tensor("wkr", [D, RD], F32, kind="ExternalInput").ap()
    w_fc = nc.dram_tensor("wfc", [UP, D], F32, kind="ExternalInput").ap()
    w_gu = nc.dram_tensor("wgu", [OGU, D], F32, kind="ExternalInput").ap()
    w_dn = nc.dram_tensor("wdn", [D, IF], F32, kind="ExternalInput").ap()
    b_d = {}
    for nm, szn in [("bdkv", DWN), ("buk", UP), ("buv", UP), ("bdq", DWN),
                    ("buq", UP), ("bqr", DWN), ("bkr", RD), ("bfc", D)]:
        b_d[nm] = nc.dram_tensor(nm, [1, szn], F32, kind="ExternalInput").ap()
    ident_d = nc.dram_tensor("ident", [128, 128], BF16, kind="ExternalInput").ap()
    perm_d = nc.dram_tensor("perm", [128, 128], BF16, kind="ExternalInput").ap()
    tabf_d = nc.dram_tensor("tabf", [128, 2 * L], F32, kind="ExternalInput").ap()
    out_d = nc.dram_tensor("out", [L, D], F32, kind="ExternalOutput").ap()
    w1t_s = nc.dram_tensor("w1t_s", [NOG1, 128, 16 * 512], BF16, kind="Internal").ap()
    w2t_s = nc.dram_tensor("w2t_s", [NOC2, 128, NIC2 * 128], BF16, kind="Internal").ap()
    o_s = nc.dram_tensor("o_s", [UP, L], BF16, kind=skind).ap()
    x2_s = nc.dram_tensor("x2_s", [L, D], BF16, kind=skind).ap()
    if DEBUG_SCRATCH:
        dbg_ckv = nc.dram_tensor("dbg_ckv", [128, 4 * L], BF16, kind="ExternalOutput").ap()
        dbg_cq = nc.dram_tensor("dbg_cq", [128, 4 * L], BF16, kind="ExternalOutput").ap()
        dbg_kr = nc.dram_tensor("dbg_kr", [32, L], BF16, kind="ExternalOutput").ap()
        dbg_lnT = nc.dram_tensor("dbg_lnT", [128, 16 * 512], BF16, kind="ExternalOutput").ap()
        dbg_ln1 = nc.dram_tensor("dbg_ln1", [128, D], BF16, kind="ExternalOutput").ap()
        dbg_xq = nc.dram_tensor("dbg_xq", [128, 64 * 128], BF16, kind="ExternalOutput").ap()
        dbg_h = nc.dram_tensor("dbg_h", [128, 4 * IF], BF16, kind="ExternalOutput").ap()
        dbg_hqT = nc.dram_tensor("dbg_hqT", [128, NIC2 * 512], BF16, kind="ExternalOutput").ap()
        dbg_y2 = nc.dram_tensor("dbg_y2", [128, 4 * D], BF16, kind="ExternalOutput").ap()
        dbg_w1a = nc.dram_tensor("dbg_w1a", [128, 16 * 512], BF16, kind="ExternalOutput").ap()
        dbg_w1b = nc.dram_tensor("dbg_w1b", [128, 16 * 512], BF16, kind="ExternalOutput").ap()

    stk = ExitStack()
    with stk:
        def sb(name, shape, dt):
            return stk.enter_context(nc.sbuf_tensor(name, shape, dt))
        def sem(name):
            return stk.enter_context(nc.semaphore(name))

        ident = sb("identt", [128, 128], BF16)
        perm = sb("permt", [128, 128], BF16)
        ones = sb("onest", [1, 512], BF16)
        stats = sb("statst", [128, 16], F32)
        epsb = sb("epsbt", [128, 1], F32)
        magb = sb("magbt", [128, 1], F32)
        stA = ExitStack()
        def sbA(name, shape, dt):
            return stA.enter_context(nc.sbuf_tensor(name, shape, dt))
        c_kv = sbA("c_kv", [128, 4 * L], BF16)
        c_q = sbA("c_qq", [128, 4 * L], BF16)
        k_r = sbA("k_rfm", [32, L], BF16)
        tabf = sbA("tabft", [128, 2 * L], F32)
        rt1 = sbA("rt1", [128, 512], F32)
        rt2 = sbA("rt2", [128, 512], F32)

        cn = {nm: C(sem(nm)) for nm in
              ["ld", "gld", "pe", "act", "dve", "gp",
               "qk", "xp", "avd", "rcp", "bcm", "fin", "odm", "out", "wst"]}
        vS = SelfW(sem("vself"), V)
        sS = SelfW(sem("sself"), S)

        def ln_stats(x_ap, width, dump_ap):
            i = V.reduce_sum(stats[:, 0:1], x_ap, AX.X)
            cn["dve"].inc(i)
            cn["dve"].wait(V)
            i = S.activation(dump_ap, x_ap, AF.Square, accum_out=stats[:, 1:2])
            cn["act"].inc(i)
            vS(V.tensor_scalar(stats[:, 2:3], stats[:, 0:1], -1.0 / width, None,
                               ALU.mult))
            vS(V.tensor_mul(stats[:, 3:4], stats[:, 2:3], stats[:, 2:3]))
            cn["act"].wait(V)
            i = V.scalar_tensor_tensor(stats[:, 4:5], stats[:, 1:2], 1.0 / width,
                                       stats[:, 3:4], ALU.mult, ALU.subtract)
            cn["dve"].inc(i)
            cn["dve"].wait(S)
            sS(S.activation(stats[:, 5:6], stats[:, 4:5], AF.Ln, bias=epsb[:]))
            i = S.activation(stats[:, 6:7], stats[:, 5:6], AF.Exp, scale=-0.5)
            cn["act"].inc(i)
            cn["act"].wait(V)

        def ln_apply(x_ap, out_ap):
            i = V.tensor_scalar(out_ap, x_ap, stats[:, 2:3], stats[:, 6:7],
                                ALU.add, ALU.mult)
            cn["dve"].inc(i)

        def quant_scales(src_ap):
            vS(V.tensor_reduce(stats[:, 7:8], src_ap, AX.X, ALU.max,
                               apply_absolute_value=True))
            vS(V.tensor_scalar(stats[:, 8:9], stats[:, 7:8], 1.0 / 127.0, 1e-8,
                               ALU.mult, ALU.max))
            i = V.reciprocal(stats[:, 9:10], stats[:, 8:9])
            vS(i)
            cn["dve"].inc(V.engine_nop())

        def quant_apply(src_ap, scr_ap, dst_ap):
            cn["dve"].wait(S)
            i = S.activation(scr_ap, src_ap, AF.Identity,
                             bias=magb[:], scale=stats[:, 9:10])
            cn["act"].inc(i)
            cn["act"].wait(V)
            i = V.tensor_scalar(dst_ap, scr_ap, -MAGIC, stats[:, 8:9],
                                ALU.add, ALU.mult)
            cn["dve"].inc(i)

        def rope7(pre_ap, rot_ap, out_ap, rows, col0, width):
            t1 = rt1[0:rows, 0:width]
            t2 = rt2[0:rows, 0:width]
            sl = (slice(0, rows), slice(col0, col0 + width))
            cl = (slice(0, rows), slice(L + col0, L + col0 + width))
            V.tensor_mul(t1, pre_ap, tabf[sl])
            vS(V.tensor_mul(t2, rot_ap, tabf[cl]))
            i = V.tensor_add(out_ap, t1, t2)
            cn["dve"].inc(i)

        st1 = ExitStack()
        def sb1(name, shape, dt):
            return st1.enter_context(nc.sbuf_tensor(name, shape, dt))
        wdkv = sb1("swdkv", [128, 16 * 512], BF16)
        wdq = sb1("swdq", [128, 16 * 512], BF16)
        wkr = sb1("swkr", [128, 16 * 32], BF16)
        bdkv = sb1("sbdkv", [1, DWN], BF16)
        bdq = sb1("sbdq", [1, DWN], BF16)
        bkr = sb1("sbkr", [1, RD], BF16)
        xt = sb1("xt", [128, D], F32)
        ln1 = sb1("ln1", [128, D], BF16)
        lnT = sb1("lnT", [128, 16 * 512], BF16)
        prb = sb1("prb", [32, 512], BF16)
        rob = sb1("rob", [32, 512], BF16)
        ps_tA = st1.enter_context(nc.psum_tensor("ps_tA", [128, 128], BF16))
        ps_tB = st1.enter_context(nc.psum_tensor("ps_tB", [128, 128], BF16))
        ps_pA = st1.enter_context(nc.psum_tensor("ps_pA", [128, 512], F32))
        ps_pB = st1.enter_context(nc.psum_tensor("ps_pB", [128, 512], F32))
        ps_kr = st1.enter_context(nc.psum_tensor("ps_kr", [32, 512], F32))

        def emit_ph1():
            cn["ld"].inc(SY.dma_start(out=ident[:], in_=ident_d), 16)
            cn["ld"].inc(SY.dma_start(out=perm[:], in_=perm_d), 16)
            cn["ld"].inc(SY.dma_start(out=tabf[:], in_=tabf_d), 16)
            cn["gld"].inc(G.dma_start(
                out=wdkv[:].rearrange("p (c n) -> p c n", c=16),
                in_=w_dkv.rearrange("(c p) n -> p c n", p=128)), 16)
            cn["gld"].inc(G.dma_start(
                out=wdq[:].rearrange("p (c n) -> p c n", c=16),
                in_=w_dq.rearrange("(c p) n -> p c n", p=128)), 16)
            cn["gld"].inc(G.dma_start(
                out=wkr[:].rearrange("p (c n) -> p c n", c=16),
                in_=w_kr.rearrange("(c p) n -> p c n", p=128)), 16)
            cn["gld"].inc(G.dma_start(out=bdkv[:], in_=b_d["bdkv"]), 16)
            cn["gld"].inc(G.dma_start(out=bdq[:], in_=b_d["bdq"]), 16)
            cn["gld"].inc(G.dma_start(out=bkr[:], in_=b_d["bkr"]), 16)
            V.memset(ones[:], 1.0)
            V.memset(epsb[:], 1e-5)
            V.memset(magb[:], MAGIC)
            cn["dve"].inc(V.engine_nop())
            for eng in (T, V, S):
                cn["ld"].wait(eng)
                cn["gld"].wait(eng)
            cn["dve"].wait(T)

            ev = []
            pj = []
            for st_i in range(NST):
                col0 = st_i * 512
                for ts in range(4):
                    cn["pe"].wait(G)
                    cn["dve"].wait(G)
                    cn["act"].wait(G)
                    r0 = (st_i * 4 + ts) * 128
                    cn["gld"].inc(G.dma_start(out=xt[:], in_=x_d[r0:r0 + 128, :]), 16)
                    cn["gld"].wait(V)
                    cn["gld"].wait(S)
                    cn["pe"].wait(S)
                    cn["pe"].wait(V)
                    ln_stats(xt[:], D, ln1[:])
                    ln_apply(xt[:], ln1[:])
                    cn["dve"].wait(T)
                    for c in range(16):
                        pst = ps_tA if c % 2 == 0 else ps_tB
                        if len(ev) >= 2:
                            T.wait_ge(cn["act"].sem, ev[-2])
                        i = T.transpose(pst[:], ln1[:, c * 128:(c + 1) * 128], ident[:])
                        cn["pe"].inc(i)
                        cn["pe"].wait(S)
                        i = S.copy(lnT[:, c * 512 + ts * 128:c * 512 + ts * 128 + 128],
                                   pst[:])
                        cn["act"].inc(i)
                        ev.append(cn["act"].n)
                cn["act"].wait(T)
                for (wt, bt, dst) in ((wdkv, bdkv, c_kv), (wdq, bdq, c_q)):
                    for m in range(4):
                        psp = ps_pA if m % 2 == 0 else ps_pB
                        if len(pj) >= 2:
                            T.wait_ge(cn["act"].sem, pj[-2])
                        for c in range(16):
                            T.matmul(psp[:],
                                     wt[:, c * 512 + m * 128:c * 512 + m * 128 + 128],
                                     lnT[:, c * 512:(c + 1) * 512],
                                     start=(c == 0), stop=False)
                        i = T.matmul(psp[:], bt[0:1, m * 128:(m + 1) * 128],
                                     ones[0:1, :], start=False, stop=True)
                        cn["pe"].inc(i)
                        cn["pe"].wait(S)
                        i = S.copy(dst[:, m * L + col0:m * L + col0 + 512], psp[:])
                        cn["act"].inc(i)
                        pj.append(cn["act"].n)
                cn["act"].wait(T)
                cn["dve"].wait(T)
                for c in range(16):
                    T.matmul(ps_kr[:], wkr[:, c * 32:(c + 1) * 32],
                             lnT[:, c * 512:(c + 1) * 512],
                             start=(c == 0), stop=False)
                i = T.matmul(ps_kr[:], bkr[0:1, :], ones[0:1, :],
                             start=False, stop=True)
                cn["pe"].inc(i)
                cn["pe"].wait(S)
                cn["dve"].wait(S)
                i = S.copy(prb[:], ps_kr[:])
                cn["act"].inc(i)
                cn["act"].wait(T)
                i = T.matmul(ps_kr[:], perm[0:32, 0:32], prb[:], start=True, stop=True)
                cn["pe"].inc(i)
                cn["pe"].wait(S)
                i = S.copy(rob[:], ps_kr[:])
                cn["act"].inc(i)
                cn["act"].wait(V)
                rope7(prb[:], rob[:], k_r[:, col0:col0 + 512], 32, col0, 512)

        with nc.Block("ph1") as blk:
            @blk.sync
            def _(sy):
                emit_ph1()
                if DEBUG_SCRATCH:
                    cn["dve"].wait(SY)
                    cn["act"].wait(SY)
                    cn["out"].inc(SY.dma_start(out=dbg_ckv, in_=c_kv[:]), 16)
                    cn["out"].inc(SY.dma_start(out=dbg_cq, in_=c_q[:]), 16)
                    cn["out"].inc(SY.dma_start(out=dbg_kr, in_=k_r[:]), 16)
                    cn["out"].inc(SY.dma_start(out=dbg_lnT, in_=lnT[:]), 16)
                    cn["out"].inc(SY.dma_start(out=dbg_ln1, in_=ln1[:]), 16)
                    SY.wait_ge(cn["out"].sem, cn["out"].n)
        st1.close()

        if DEBUG_PHASE >= 2:
            st2 = ExitStack()
            def sb2(name, shape, dt):
                return st2.enter_context(nc.sbuf_tensor(name, shape, dt))
            w1l = sb2("w1l", [128, D], F32)
            w1q = sb2("w1q", [128, D], BF16)
            stg1 = sb2("stg1", [128, 16 * 512], BF16)
            w2l = sb2("w2l", [128, IF], F32)
            w2q = sb2("w2q", [128, IF], BF16)
            stg2 = sb2("stg2", [128, NIC2 * 128], BF16)
            qscr = sb2("qscr", [128, D], F32)
            ps_qA = st2.enter_context(nc.psum_tensor("ps_qA", [128, 128], BF16))
            ps_qB = st2.enter_context(nc.psum_tensor("ps_qB", [128, 128], BF16))

            def emit_prep():
                tev = []
                for og in range(NOG1):
                    for b in range(4):
                        r0 = og * 512 + b * 128
                        cn["dve"].wait(SY)
                        cn["act"].wait(SY)
                        cn["pe"].wait(SY)
                        cn["ld"].inc(SY.dma_start(out=w1l[:],
                                                  in_=w_gu[r0:r0 + 128, :]), 16)
                        cn["ld"].wait(V)
                        cn["ld"].wait(S)
                        quant_scales(w1l[:])
                        cn["pe"].wait(V)
                        quant_apply(w1l[:], qscr[:], w1q[:])
                        cn["dve"].wait(T)
                        for c in range(16):
                            psq = ps_qA if c % 2 == 0 else ps_qB
                            if len(tev) >= 2:
                                T.wait_ge(cn["act"].sem, tev[-2])
                            i = T.transpose(psq[:], w1q[:, c * 128:(c + 1) * 128],
                                            ident[:])
                            cn["pe"].inc(i)
                            cn["pe"].wait(S)
                            i = S.copy(stg1[:, c * 512 + b * 128:
                                            c * 512 + b * 128 + 128], psq[:])
                            cn["act"].inc(i)
                            tev.append(cn["act"].n)
                    cn["act"].wait(G)
                    cn["gp"].inc(G.dma_start(out=w1t_s[og], in_=stg1[:]), 16)
                    cn["gp"].wait(S)
                for oc in range(NOC2):
                    cn["dve"].wait(SY)
                    cn["act"].wait(SY)
                    cn["pe"].wait(SY)
                    cn["ld"].inc(SY.dma_start(out=w2l[:],
                                              in_=w_dn[oc * 128:(oc + 1) * 128, :]), 16)
                    cn["ld"].wait(V)
                    cn["ld"].wait(S)
                    quant_scales(w2l[:])
                    cn["pe"].wait(V)
                    cn["pe"].wait(S)
                    cn["dve"].wait(S)
                    i = S.activation(w2l[:], w2l[:], AF.Identity,
                                     bias=magb[:], scale=stats[:, 9:10])
                    cn["act"].inc(i)
                    cn["act"].wait(V)
                    i = V.tensor_scalar(w2q[:], w2l[:], -MAGIC, stats[:, 8:9],
                                        ALU.add, ALU.mult)
                    cn["dve"].inc(i)
                    cn["dve"].wait(T)
                    tev2 = []
                    for c in range(NIC2):
                        psq = ps_qA if c % 2 == 0 else ps_qB
                        if len(tev2) >= 2:
                            T.wait_ge(cn["act"].sem, tev2[-2])
                        i = T.transpose(psq[:], w2q[:, c * 128:(c + 1) * 128], ident[:])
                        cn["pe"].inc(i)
                        cn["pe"].wait(S)
                        i = S.copy(stg2[:, c * 128:(c + 1) * 128], psq[:])
                        cn["act"].inc(i)
                        tev2.append(cn["act"].n)
                    cn["act"].wait(G)
                    cn["gp"].inc(G.dma_start(out=w2t_s[oc], in_=stg2[:]), 16)
                    cn["gp"].wait(S)

            with nc.Block("prep") as blk:
                @blk.sync
                def _(sy):
                    emit_prep()
            st2.close()

        if DEBUG_PHASE >= 3:
            st3 = ExitStack()
            def sb3(name, shape, dt):
                return st3.enter_context(nc.sbuf_tensor(name, shape, dt))
            wuk = sb3("swuk", [128, 4 * UP], BF16)
            wuv = sb3("swuv", [128, 4 * UP], BF16)
            wuq = sb3("swuq", [128, 4 * UP], BF16)
            wqr = sb3("swqr", [128, 4 * DWN], BF16)
            buk = sb3("sbuk", [1, UP], BF16)
            buv = sb3("sbuv", [1, UP], BF16)
            buq = sb3("sbuq", [1, UP], BF16)
            bqr = sb3("sbqr", [1, DWN], BF16)
            K_h = sb3("K_h", [96, L], BF16)
            V_h = sb3("V_h", [128, NKV * 65], BF16)
            q_h = sb3("q_h", [96, L], BF16)
            q_r4 = sb3("q_r4", [128, L], BF16)
            pre4 = sb3("pre4", [128, L], BF16)
            ro4A = sb3("ro4A", [128, 512], BF16)
            ro4B = sb3("ro4B", [128, 512], BF16)
            ptA = sb3("ptA", [128, 512], BF16)
            ptB = sb3("ptB", [128, 512], BF16)
            osbA = sb3("osbA", [64, 512], BF16)
            osbB = sb3("osbB", [64, 512], BF16)
            linv = sb3("linv", [1, 512], F32)
            onesf = sb3("onesf", [1, 64], F32)
            lbc = sb3("lbc", [64, 512], F32)
            ps_sA = st3.enter_context(nc.psum_tensor("ps_sA", [128, 512], F32))
            ps_sB = st3.enter_context(nc.psum_tensor("ps_sB", [128, 512], F32))
            ps_oA = st3.enter_context(nc.psum_tensor("ps_oA", [128, 512], F32))
            ps_oB = st3.enter_context(nc.psum_tensor("ps_oB", [128, 512], F32))
            ps_lb = st3.enter_context(nc.psum_tensor("ps_lb", [64, 512], F32))
            ps_bA = st3.enter_context(nc.psum_tensor("ps_bA", [128, 512], F32))
            ps_bB = st3.enter_context(nc.psum_tensor("ps_bB", [128, 512], F32))
            ps_s = [ps_sA, ps_sB]
            ps_o = [ps_oA, ps_oB]
            pt = [ptA, ptB]
            osb = [osbA, osbB]
            ps_bd = [ps_bA, ps_bB]

            def emit_ph2():
                cn["gld"].inc(G.dma_start(
                    out=wuk[:].rearrange("p (c n) -> p c n", c=4),
                    in_=w_uk.rearrange("(c p) n -> p c n", p=128)), 16)
                cn["gld"].inc(G.dma_start(
                    out=wuv[:].rearrange("p (c n) -> p c n", c=4),
                    in_=w_uv.rearrange("(c p) n -> p c n", p=128)), 16)
                cn["gld"].inc(G.dma_start(
                    out=wuq[:].rearrange("p (c n) -> p c n", c=4),
                    in_=w_uq.rearrange("(c p) n -> p c n", p=128)), 16)
                cn["gld"].inc(G.dma_start(
                    out=wqr[:].rearrange("p (c n) -> p c n", c=4),
                    in_=w_qr.rearrange("(c p) n -> p c n", p=128)), 16)
                cn["gld"].inc(G.dma_start(out=buk[:], in_=b_d["buk"]), 16)
                cn["gld"].inc(G.dma_start(out=buv[:], in_=b_d["buv"]), 16)
                cn["gld"].inc(G.dma_start(out=buq[:], in_=b_d["buq"]), 16)
                cn["gld"].inc(G.dma_start(out=bqr[:], in_=b_d["bqr"]), 16)
                for eng in (T, V, S):
                    cn["gld"].wait(eng)
                V.memset(onesf[:], 1.0)
                for t in range(NKV):
                    V.memset(V_h[:, t * 65 + 64:t * 65 + 65], 1.0)
                cn["dve"].inc(V.engine_nop())
                cn["dve"].wait(T)

                bmk = []

                def build_fm64(src, wt, bt, h0, dst, row0, scale_imm):
                    for n in range(8):
                        psp = ps_bd[n % 2]
                        if len(bmk) >= 2:
                            T.wait_ge(cn["act"].sem, bmk[-2])
                        for c in range(4):
                            T.matmul(psp[0:64, :],
                                     wt[:, c * UP + h0:c * UP + h0 + 64],
                                     src[:, c * L + n * 512:c * L + n * 512 + 512],
                                     start=(c == 0), stop=False)
                        i = T.matmul(psp[0:64, :], bt[0:1, h0:h0 + 64],
                                     ones[0:1, :], start=False, stop=True)
                        cn["pe"].inc(i)
                        cn["pe"].wait(S)
                        i = S.activation(dst[row0:row0 + 64, n * 512:(n + 1) * 512],
                                         psp[0:64, :], AF.Identity, scale=scale_imm)
                        cn["act"].inc(i)
                        bmk.append(cn["act"].n)

                def emit_fin_pe(I):
                    T.wait_ge(cn["fin"].sem, I)
                    T.wait_ge(cn["rcp"].sem, I + 1)
                    i = T.matmul(ps_lb[:], onesf[0:1, :], linv[0:1, :],
                                 start=True, stop=True)
                    cn["bcm"].inc(i)

                def emit_fin_rest(I):
                    h_i, qs_i = I // NST, I % NST
                    V.wait_ge(cn["bcm"].sem, I + 1)
                    vS(V.tensor_copy(lbc[:], ps_lb[:]))
                    if I >= 2:
                        V.wait_ge(cn["odm"].sem, 16 * (I - 1))
                    i = V.tensor_mul(osb[I % 2][:], ps_o[I % 2][0:64, :], lbc[:])
                    cn["fin"].inc(i)
                    G.wait_ge(cn["fin"].sem, I + 1)
                    i = G.dma_start(out=o_s[h_i * 64:(h_i + 1) * 64,
                                            qs_i * 512:(qs_i + 1) * 512],
                                    in_=osb[I % 2][:])
                    cn["odm"].inc(i, 16)

                gidx = 0
                for h in range(NH):
                    build_fm64(c_kv, wuk, buk, h * 64, K_h, 0, 1.0)
                    V.wait_ge(cn["qk"].sem, h * 256) if h > 0 else None
                    cn["dve"].inc(V.tensor_copy(K_h[64:96, :], k_r[:]))
                    for t in range(NKV):
                        psp = ps_bd[t % 2]
                        if len(bmk) >= 2:
                            T.wait_ge(cn["act"].sem, bmk[-2])
                        for c in range(4):
                            T.matmul(psp[:, 0:64],
                                     c_kv[:, c * L + t * 128:c * L + t * 128 + 128],
                                     wuv[:, c * UP + h * 64:c * UP + h * 64 + 64],
                                     start=(c == 0), stop=False)
                        i = T.matmul(psp[:, 0:64], ones[0:1, 0:128],
                                     buv[0:1, h * 64:h * 64 + 64],
                                     start=False, stop=True)
                        cn["pe"].inc(i)
                        cn["pe"].wait(S)
                        i = S.copy(V_h[:, t * 65:t * 65 + 64], psp[:, 0:64])
                        cn["act"].inc(i)
                        bmk.append(cn["act"].n)
                    if h % 4 == 0:
                        g4 = h // 4
                        cn["gld"].wait(V)
                        cn["gld"].wait(S)
                        for n in range(8):
                            psp = ps_bd[n % 2]
                            if len(bmk) >= 2:
                                T.wait_ge(cn["act"].sem, bmk[-2])
                            for c in range(4):
                                T.matmul(psp[:],
                                         wqr[:, c * DWN + g4 * 128:
                                             c * DWN + g4 * 128 + 128],
                                         c_q[:, c * L + n * 512:c * L + n * 512 + 512],
                                         start=(c == 0), stop=False)
                            i = T.matmul(psp[:], bqr[0:1, g4 * 128:g4 * 128 + 128],
                                         ones[0:1, :], start=False, stop=True)
                            cn["pe"].inc(i)
                            cn["pe"].wait(S)
                            i = S.activation(pre4[:, n * 512:(n + 1) * 512], psp[:],
                                             AF.Identity, scale=SCALE)
                            cn["act"].inc(i)
                            bmk.append(cn["act"].n)
                        for n in range(8):
                            psp = ps_bd[n % 2]
                            rb = ro4A if n % 2 == 0 else ro4B
                            if len(bmk) >= 2:
                                T.wait_ge(cn["act"].sem, bmk[-2])
                            i = T.matmul(psp[:], perm[:],
                                         pre4[:, n * 512:(n + 1) * 512],
                                         start=True, stop=True)
                            cn["pe"].inc(i)
                            cn["pe"].wait(S)
                            i = S.copy(rb[:], psp[:])
                            cn["act"].inc(i)
                            bmk.append(cn["act"].n)
                            cn["act"].wait(V)
                            rope7(pre4[:, n * 512:(n + 1) * 512], rb[:],
                                  q_r4[:, n * 512:(n + 1) * 512], 128, n * 512, 512)
                    build_fm64(c_q, wuq, buq, h * 64, q_h, 0, SCALE)
                    cn["dve"].wait(G)
                    cn["act"].wait(G)
                    i = G.dma_start(out=q_h[64:96, :],
                                    in_=q_r4[(h % 4) * 32:(h % 4) * 32 + 32, :])
                    cn["gld"].inc(i, 16)
                    cn["gld"].wait(T)
                    cn["act"].wait(T)
                    cn["dve"].wait(T)

                    for qs in range(NST):
                        I = gidx
                        for c in range(NKV):
                            g = I * NKV + c
                            if g >= 2:
                                T.wait_ge(cn["xp"].sem, g - 1)
                            i = T.matmul(ps_s[c % 2][:],
                                         K_h[:, c * 128:(c + 1) * 128],
                                         q_h[:, qs * 512:(qs + 1) * 512],
                                         start=True, stop=True)
                            cn["qk"].inc(i)
                            S.wait_ge(cn["qk"].sem, g + 1)
                            i = S.activation(pt[c % 2][:], ps_s[c % 2][:], AF.Exp)
                            cn["xp"].inc(i)
                            if c == 1 and I >= 2:
                                T.wait_ge(cn["fin"].sem, I - 1)
                            if c >= 1:
                                cc = c - 1
                                T.wait_ge(cn["xp"].sem, I * NKV + cc + 1)
                                T.matmul(ps_o[I % 2][0:65, :],
                                         V_h[:, cc * 65:cc * 65 + 65],
                                         pt[cc % 2][:],
                                         start=(cc == 0), stop=False)
                            if c == 10 and I >= 1:
                                emit_fin_pe(I - 1)
                            if c == 14 and I >= 1:
                                emit_fin_rest(I - 1)
                        T.wait_ge(cn["xp"].sem, I * NKV + NKV)
                        i = T.matmul(ps_o[I % 2][0:65, :],
                                     V_h[:, 31 * 65:31 * 65 + 65],
                                     pt[1][:], start=False, stop=True)
                        cn["avd"].inc(i)
                        V.wait_ge(cn["avd"].sem, I + 1)
                        if I >= 1:
                            V.wait_ge(cn["bcm"].sem, I)
                        i = V.reciprocal(linv[:], ps_o[I % 2][64:65, :])
                        cn["rcp"].inc(i)
                        gidx += 1
                emit_fin_pe(gidx - 1)
                emit_fin_rest(gidx - 1)

            with nc.Block("ph2") as blk:
                @blk.sync
                def _(sy):
                    emit_ph2()
            st3.close()

        stA.close()

        if DEBUG_PHASE >= 4:
            st4 = ExitStack()
            def sb4(name, shape, dt):
                return st4.enter_context(nc.sbuf_tensor(name, shape, dt))
            wfc = sb4("swfc", [128, 8 * D], BF16)
            bfc = sb4("sbfc", [1, D], BF16)
            ot = sb4("ot", [128, 8 * 512], BF16)
            xt3 = sb4("xt3", [128, D], F32)
            fcs = sb4("fcs", [128, D], F32)
            lns = sb4("lns", [128, D], F32)
            x2sb = sb4("x2sb", [128, D], F32)
            dmp3 = sb4("dmp3", [128, D], BF16)
            ps_fA = st4.enter_context(nc.psum_tensor("ps_fA", [128, 2048], F32))
            ps_fB = st4.enter_context(nc.psum_tensor("ps_fB", [128, 2048], F32))

            def emit_ph3():
                cn["gld"].inc(G.dma_start(
                    out=wfc[:].rearrange("p (c n) -> p c n", c=8),
                    in_=w_fc.rearrange("(c p) n -> p c n", p=128)), 16)
                cn["gld"].inc(G.dma_start(out=bfc[:], in_=b_d["bfc"]), 16)
                for eng in (T, V, S):
                    cn["gld"].wait(eng)
                fcev = []
                for st_i in range(NST):
                    cn["pe"].wait(SY)
                    for c in range(8):
                        i = SY.dma_start(out=ot[:, c * 512:(c + 1) * 512],
                                         in_=o_s[c * 128:(c + 1) * 128,
                                                 st_i * 512:(st_i + 1) * 512])
                        cn["ld"].inc(i, 16)
                    cn["ld"].wait(T)
                    for ts in range(4):
                        psf = ps_fA if ts % 2 == 0 else ps_fB
                        r0 = (st_i * 4 + ts) * 128
                        cn["dve"].wait(G)
                        cn["gld"].inc(G.dma_start(out=xt3[:],
                                                  in_=x_d[r0:r0 + 128, :]), 16)
                        if len(fcev) >= 2:
                            T.wait_ge(cn["act"].sem, fcev[-2])
                        for og in range(4):
                            for c in range(8):
                                T.matmul(psf[:, og * 512:(og + 1) * 512],
                                         ot[:, c * 512 + ts * 128:
                                            c * 512 + ts * 128 + 128],
                                         wfc[:, c * D + og * 512:
                                             c * D + og * 512 + 512],
                                         start=(c == 0), stop=False)
                            i = T.matmul(psf[:, og * 512:(og + 1) * 512],
                                         ones[0:1, 0:128],
                                         bfc[0:1, og * 512:(og + 1) * 512],
                                         start=False, stop=True)
                            cn["pe"].inc(i)
                        cn["pe"].wait(S)
                        cn["dve"].wait(S)
                        i = S.copy(fcs[:], psf[:])
                        cn["act"].inc(i)
                        fcev.append(cn["act"].n)
                        cn["act"].wait(V)
                        ln_stats(fcs[:], D, dmp3[:])
                        ln_apply(fcs[:], lns[:])
                        cn["dve"].wait(V)
                        cn["gld"].wait(V)
                        cn["gp"].wait(V)
                        vS(V.tensor_add(x2sb[:], xt3[:], lns[:]))
                        cn["dve"].inc(V.engine_nop())
                        cn["dve"].wait(G)
                        i = G.dma_start(out=x2_s[r0:r0 + 128, :], in_=x2sb[:])
                        cn["gp"].inc(i, 16)

            with nc.Block("ph3") as blk:
                @blk.sync
                def _(sy):
                    emit_ph3()
            st4.close()

        if DEBUG_PHASE >= 5:
            st5 = ExitStack()
            def sb5(name, shape, dt):
                return st5.enter_context(nc.sbuf_tensor(name, shape, dt))
            x2t = sb5("x2t", [128, D], BF16)
            scr4 = sb5("scr4", [128, D], F32)
            xq = sb5("xq", [128, D], BF16)
            xqT = sb5("xqT", [128, 64 * 128], BF16)
            w1sA = sb5("w1sA", [128, 16 * 512], BF16)
            w1sB = sb5("w1sB", [128, 16 * 512], BF16)
            gsb = sb5("gsb", [128, 2048], BF16)
            hbuf = sb5("hbuf", [128, 4 * IF], BF16)
            hqT = sb5("hqT", [128, NIC2 * 512], BF16)
            w2sA = sb5("w2sA", [128, NIC2 * 128], BF16)
            w2sB = sb5("w2sB", [128, NIC2 * 128], BF16)
            y2f = sb5("y2f", [128, 512], BF16)
            y2tm = sb5("y2tm", [128, 4 * D], BF16)
            qsc = sb5("qsc", [128, 512], F32)
            ps_a = st5.enter_context(nc.psum_tensor("ps_a", [128, 2048], F32))
            ps_b = st5.enter_context(nc.psum_tensor("ps_b", [128, 2048], F32))
            psa_bf = ps_a[:].bitcast(BF16)
            psb_bf = ps_b[:].bitcast(BF16)
            w1s = [w1sA, w1sB]
            w2s = [w2sA, w2sB]

            def emit_ph4():
                stg = DEBUG_P4STAGE
                for tt in range(NST):
                    for ts in range(4):
                        r0 = (tt * 4 + ts) * 128
                        cn["dve"].wait(G)
                        cn["act"].wait(G)
                        cn["gld"].inc(G.dma_start(out=x2t[:],
                                                  in_=x2_s[r0:r0 + 128, :]), 16)
                        cn["gld"].wait(V)
                        cn["gld"].wait(S)
                        ln_stats(x2t[:], D, xq[:])
                        V.tensor_reduce(stats[:, 12:13], x2t[:], AX.X, ALU.max)
                        vS(V.tensor_reduce(stats[:, 13:14], x2t[:], AX.X, ALU.min))
                        V.tensor_scalar(stats[:, 14:15], stats[:, 12:13],
                                        stats[:, 2:3], None, ALU.add)
                        vS(V.tensor_scalar(stats[:, 15:16], stats[:, 13:14],
                                           stats[:, 2:3], -1.0, ALU.add, ALU.mult))
                        vS(V.tensor_max(stats[:, 7:8], stats[:, 14:15],
                                        stats[:, 15:16]))
                        vS(V.tensor_mul(stats[:, 7:8], stats[:, 7:8], stats[:, 6:7]))
                        vS(V.tensor_scalar(stats[:, 8:9], stats[:, 7:8], 1.0 / 127.0,
                                           1e-8, ALU.mult, ALU.max))
                        vS(V.reciprocal(stats[:, 9:10], stats[:, 8:9]))
                        vS(V.tensor_mul(stats[:, 10:11], stats[:, 6:7],
                                        stats[:, 9:10]))
                        i = V.tensor_scalar(stats[:, 11:12], stats[:, 2:3],
                                            stats[:, 10:11], MAGIC, ALU.mult, ALU.add)
                        vS(i)
                        cn["dve"].inc(V.engine_nop())
                        cn["dve"].wait(S)
                        cn["out"].wait(S)
                        i = S.activation(scr4[:], x2t[:], AF.Identity,
                                         bias=stats[:, 11:12], scale=stats[:, 10:11])
                        cn["act"].inc(i)
                        cn["act"].wait(V)
                        i = V.tensor_scalar(xq[:], scr4[:], -MAGIC, stats[:, 8:9],
                                            ALU.add, ALU.mult)
                        cn["dve"].inc(i)
                        cn["dve"].wait(T)
                        for c in range(16):
                            pv = psa_bf[:, 0:128] if c % 2 == 0 else psa_bf[:, 1024:1152]
                            if c >= 2:
                                T.wait_ge(cn["act"].sem, cn["act"].n - 1)
                            i = T.transpose(pv, xq[:, c * 128:(c + 1) * 128], ident[:])
                            cn["pe"].inc(i)
                            cn["pe"].wait(S)
                            i = S.copy(xqT[:, (c * 4 + ts) * 128:
                                           (c * 4 + ts) * 128 + 128], pv)
                            cn["act"].inc(i)
                    if stg < "B":
                        continue
                    cn["pe"].wait(SY)
                    wgm, wum = [], []
                    cn["wst"].inc(SY.dma_start(out=w1s[0][:], in_=w1t_s[0]), 16)
                    wgm.append(cn["wst"].n)
                    cn["wst"].inc(SY.dma_start(out=w1s[1][:], in_=w1t_s[11]), 16)
                    wum.append(cn["wst"].n)
                    gmk, umk, smk, mmk = [], [], [], []
                    for p in range(11):
                        T.wait_ge(cn["wst"].sem, wgm[p])
                        if p >= 1:
                            T.wait_ge(cn["act"].sem, smk[-1])
                        else:
                            cn["act"].wait(T)
                        last = None
                        for ts in range(4):
                            for c in range(16):
                                last = T.matmul(ps_a[:, ts * 512:(ts + 1) * 512],
                                                xqT[:, (c * 4 + ts) * 128:
                                                    (c * 4 + ts) * 128 + 128],
                                                w1s[0][:, c * 512:(c + 1) * 512],
                                                start=(c == 0), stop=(c == 15))
                        cn["pe"].inc(last)
                        gmk.append(cn["pe"].n)
                        if p < 10:
                            SY.wait_ge(cn["pe"].sem, gmk[-1])
                            cn["wst"].inc(SY.dma_start(out=w1s[0][:],
                                                       in_=w1t_s[p + 1]), 16)
                            wgm.append(cn["wst"].n)
                        S.wait_ge(cn["pe"].sem, gmk[-1])
                        if p >= 1:
                            S.wait_ge(cn["dve"].sem, mmk[-1])
                        i = S.activation(gsb[:], ps_a[:], AF.Silu)
                        cn["act"].inc(i)
                        smk.append(cn["act"].n)
                        T.wait_ge(cn["wst"].sem, wum[p])
                        if p >= 1:
                            T.wait_ge(cn["dve"].sem, mmk[-1])
                        last = None
                        for ts in range(4):
                            for c in range(16):
                                last = T.matmul(ps_b[:, ts * 512:(ts + 1) * 512],
                                                xqT[:, (c * 4 + ts) * 128:
                                                    (c * 4 + ts) * 128 + 128],
                                                w1s[1][:, c * 512:(c + 1) * 512],
                                                start=(c == 0), stop=(c == 15))
                        cn["pe"].inc(last)
                        umk.append(cn["pe"].n)
                        if p < 10:
                            SY.wait_ge(cn["pe"].sem, umk[-1])
                            cn["wst"].inc(SY.dma_start(out=w1s[1][:],
                                                       in_=w1t_s[12 + p]), 16)
                            wum.append(cn["wst"].n)
                        V.wait_ge(cn["pe"].sem, umk[-1])
                        V.wait_ge(cn["act"].sem, smk[-1])
                        for ts in range(4):
                            i = V.tensor_mul(
                                hbuf[:, ts * IF + p * 512:ts * IF + (p + 1) * 512],
                                gsb[:, ts * 512:(ts + 1) * 512],
                                ps_b[:, ts * 512:(ts + 1) * 512])
                            cn["dve"].inc(i)
                        mmk.append(cn["dve"].n)
                    if DEBUG_SCRATCH and tt == NST - 1:
                        cn["dve"].wait(G)
                        i = G.dma_start(out=dbg_h, in_=hbuf[:])
                        cn["gld"].inc(i, 16)
                        cn["gld"].wait(S)
                        cn["gld"].wait(V)
                        cn["gld"].wait(T)
                    if stg < "C":
                        continue
                    for ts in range(4):
                        hsl = hbuf[:, ts * IF:(ts + 1) * IF]
                        cn["dve"].wait(V)
                        cn["pe"].wait(S)
                        quant_scales(hsl)
                        cn["dve"].wait(S)
                        for piece in range(11):
                            pp = hbuf[:, ts * IF + piece * 512:
                                      ts * IF + (piece + 1) * 512]
                            i = S.activation(qsc[:], pp, AF.Identity,
                                             bias=magb[:], scale=stats[:, 9:10])
                            cn["act"].inc(i)
                            cn["act"].wait(V)
                            i = V.tensor_scalar(pp, qsc[:], -MAGIC, stats[:, 8:9],
                                                ALU.add, ALU.mult)
                            cn["dve"].inc(i)
                            cn["dve"].wait(S)
                        cn["dve"].wait(T)
                        for c in range(NIC2):
                            pv = (psa_bf[:, 0:128] if c % 2 == 0
                                  else psa_bf[:, 1024:1152])
                            if c >= 2:
                                T.wait_ge(cn["act"].sem, cn["act"].n - 1)
                            i = T.transpose(pv, hbuf[:, ts * IF + c * 128:
                                                     ts * IF + c * 128 + 128],
                                            ident[:])
                            cn["pe"].inc(i)
                            cn["pe"].wait(S)
                            i = S.copy(hqT[:, c * 512 + ts * 128:
                                           c * 512 + ts * 128 + 128], pv)
                            cn["act"].inc(i)
                    if stg < "E":
                        continue
                    cn["pe"].wait(SY)
                    w2m = []
                    cn["wst"].inc(SY.dma_start(out=w2s[0][:], in_=w2t_s[0]), 16)
                    w2m.append(cn["wst"].n)
                    cn["wst"].inc(SY.dma_start(out=w2s[1][:], in_=w2t_s[1]), 16)
                    w2m.append(cn["wst"].n)
                    dmk, yev, ytv = [], [], []
                    for oc in range(NOC2):
                        T.wait_ge(cn["wst"].sem, w2m[oc])
                        psy = ps_a[:, 0:512] if oc % 2 == 0 else ps_b[:, 0:512]
                        if oc == 0:
                            cn["act"].wait(T)
                        if len(yev) >= 2:
                            T.wait_ge(cn["act"].sem, yev[-2])
                        last = None
                        for c in range(NIC2):
                            last = T.matmul(psy, w2s[oc % 2][:, c * 128:(c + 1) * 128],
                                            hqT[:, c * 512:(c + 1) * 512],
                                            start=(c == 0), stop=(c == NIC2 - 1))
                        cn["pe"].inc(last)
                        dmk.append(cn["pe"].n)
                        if oc + 2 < NOC2:
                            SY.wait_ge(cn["pe"].sem, dmk[-1])
                            cn["wst"].inc(SY.dma_start(out=w2s[oc % 2][:],
                                                       in_=w2t_s[oc + 2]), 16)
                            w2m.append(cn["wst"].n)
                        S.wait_ge(cn["pe"].sem, dmk[-1])
                        if oc >= 1:
                            S.wait_ge(cn["pe"].sem, ytv[-1])
                        i = S.copy(y2f[:], psy)
                        cn["act"].inc(i)
                        yev.append(cn["act"].n)
                        T.wait_ge(cn["act"].sem, yev[-1])
                        for ts in range(4):
                            pv = (psb_bf[:, 2048:2176] if ts % 2 == 0
                                  else psb_bf[:, 3072:3200])
                            if len(ytv) >= 1 or ts >= 2:
                                cn["dve"].wait(T)
                            i = T.transpose(pv, y2f[:, ts * 128:(ts + 1) * 128],
                                            ident[:])
                            cn["pe"].inc(i)
                            cn["pe"].wait(V)
                            i = V.tensor_copy(y2tm[:, ts * D + oc * 128:
                                                  ts * D + oc * 128 + 128], pv)
                            cn["dve"].inc(i)
                        ytv.append(cn["pe"].n)
                    if stg < "F":
                        continue
                    for ts in range(4):
                        r0 = (tt * 4 + ts) * 128
                        ysl = y2tm[:, ts * D:(ts + 1) * D]
                        cn["act"].wait(G)
                        cn["dve"].wait(G)
                        cn["gld"].inc(G.dma_start(out=x2t[:],
                                                  in_=x2_s[r0:r0 + 128, :]), 16)
                        cn["out"].wait(S)
                        cn["out"].wait(V)
                        cn["dve"].wait(V)
                        cn["pe"].wait(S)
                        ln_stats(ysl, D, xq[:])
                        ln_apply(ysl, scr4[:])
                        cn["dve"].wait(V)
                        cn["gld"].wait(V)
                        vS(V.tensor_add(scr4[:], scr4[:], x2t[:]))
                        cn["dve"].inc(V.engine_nop())
                        cn["dve"].wait(SY)
                        i = SY.dma_start(out=out_d[r0:r0 + 128, :], in_=scr4[:])
                        cn["out"].inc(i, 16)
                if DEBUG_SCRATCH:
                    cn["dve"].wait(SY)
                    cn["act"].wait(SY)
                    cn["out"].inc(SY.dma_start(out=dbg_w1a, in_=w1t_s[0]), 16)
                    cn["out"].inc(SY.dma_start(out=dbg_w1b, in_=w1t_s[11]), 16)
                    cn["out"].inc(SY.dma_start(out=dbg_xq, in_=xqT[:]), 16)
                    cn["out"].inc(SY.dma_start(out=dbg_hqT, in_=hqT[:]), 16)
                    cn["out"].inc(SY.dma_start(out=dbg_y2, in_=y2tm[:]), 16)
                if cn["out"].n:
                    SY.wait_ge(cn["out"].sem, cn["out"].n)

            with nc.Block("ph4") as blk:
                @blk.sync
                def _(sy):
                    emit_ph4()
            st5.close()
    return nc


def _host_consts():
    bf = ml_dtypes.bfloat16
    ident = np.eye(128, dtype=bf)
    p32 = np.zeros((32, 32), dtype=np.float32)
    for i in range(16):
        p32[2 * i + 1, 2 * i] = -1.0
        p32[2 * i, 2 * i + 1] = 1.0
    perm = np.zeros((128, 128), dtype=np.float32)
    for b in range(4):
        perm[b * 32:(b + 1) * 32, b * 32:(b + 1) * 32] = p32
    perm = perm.astype(bf)
    theta = np.exp(np.arange(0, RD, 2, dtype=np.float64) * (-np.log(10000.0) / RD))
    ang = np.arange(L, dtype=np.float64)[None, :] * theta[:, None]
    sinr = np.repeat(np.sin(ang), 2, axis=0)
    cosr = np.repeat(np.cos(ang), 2, axis=0)
    sin4 = np.tile(sinr, (4, 1))
    cos4 = np.tile(cosr, (4, 1))
    tab = np.concatenate([sin4, cos4], axis=1).astype(np.float32)
    return ident, perm, np.ascontiguousarray(tab)


_NC_CACHE = {}


def kernel(**inputs):
    key = (DEBUG_PHASE, DEBUG_SCRATCH, DEBUG_P4STAGE)
    if key not in _NC_CACHE:
        _NC_CACHE[key] = build_nc()
    nc = _NC_CACHE[key]
    ident, perm, tabf = _host_consts()
    x = np.ascontiguousarray(np.asarray(inputs["L0_attn_h"], dtype=np.float32))
    common = {
        "wdkv": inputs["L0_attn_W_dkv"], "wuk": inputs["L0_attn_W_uk"],
        "wuv": inputs["L0_attn_W_uv"], "wdq": inputs["L0_attn_W_dq"],
        "wuq": inputs["L0_attn_W_uq"], "wqr": inputs["L0_attn_W_qr"],
        "wkr": inputs["L0_attn_W_kr"], "wfc": inputs["L0_attn_W_fc"],
        "wgu": inputs["L0_ffn_w_gate_up"], "wdn": inputs["L0_ffn_w_down"],
        "bdkv": np.asarray(inputs["L0_attn_b_dkv"], np.float32).reshape(1, -1),
        "buk": np.asarray(inputs["L0_attn_b_uk"], np.float32).reshape(1, -1),
        "buv": np.asarray(inputs["L0_attn_b_uv"], np.float32).reshape(1, -1),
        "bdq": np.asarray(inputs["L0_attn_b_dq"], np.float32).reshape(1, -1),
        "buq": np.asarray(inputs["L0_attn_b_uq"], np.float32).reshape(1, -1),
        "bqr": np.asarray(inputs["L0_attn_b_qr"], np.float32).reshape(1, -1),
        "bkr": np.asarray(inputs["L0_attn_b_kr"], np.float32).reshape(1, -1),
        "bfc": np.asarray(inputs["L0_attn_b_fc"], np.float32).reshape(1, -1),
        "ident": ident, "perm": perm, "tabf": tabf,
    }
    common = {k: np.ascontiguousarray(np.asarray(v, dtype=np.float32))
              if np.asarray(v).dtype == np.float32 or k.startswith(("w", "b"))
              else v for k, v in common.items()}
    common["ident"], common["perm"], common["tabf"] = ident, perm, tabf
    in_maps = [dict(common, x=np.ascontiguousarray(x[i])) for i in range(8)]
    res = run_bass_kernel_spmd(nc, in_maps, core_ids=list(range(8)))
    out = np.stack([r["out"] for r in res.results], axis=0).astype(np.float32)
    if DEBUG_SCRATCH:
        kernel.debug_results = res.results
    return out


# version 02 (final)
# speedup vs baseline: 1.1341x; 1.1341x over previous
import numpy as np
import ml_dtypes
import concourse.bass as bass
import concourse.mybir as mybir
from concourse.bass_utils import run_bass_kernel_spmd
from contextlib import ExitStack

F32 = mybir.dt.float32
BF16 = mybir.dt.bfloat16
AF = mybir.ActivationFunctionType
ALU = mybir.AluOpType
AX = mybir.AxisListType

L = 4096
D = 2048
DWN = 512
UP = 1024
NH = 16
RD = 32
VD = 64
IF = 5632
OGU = 11264
SCALE = float(1.0 / (16 ** 0.5 + 32 ** 0.5))
MAGIC = 12582912.0
NST = 8
NKV = 32
NOG1 = 22
NOC2 = 16
NIC2 = 44

DEBUG_PHASE = 99
DEBUG_P4STAGE = "F"
DEBUG_SCRATCH = False


class C:
    __slots__ = ("sem", "n")
    def __init__(self, sem):
        self.sem = sem
        self.n = 0
    def inc(self, instr, k=1):
        instr.then_inc(self.sem, k)
        self.n += k
        return self.n
    def wait(self, eng, val=None):
        v = self.n if val is None else val
        if v > 0:
            eng.wait_ge(self.sem, v)


class SelfW:
    __slots__ = ("sem", "eng", "n")
    def __init__(self, sem, eng):
        self.sem = sem
        self.eng = eng
        self.n = 0
    def __call__(self, instr):
        self.n += 1
        instr.then_inc(self.sem, 1)
        self.eng.wait_ge(self.sem, self.n)


def build_nc():
    nc = bass.Bass("TRN2", target_bir_lowering=False)
    V, S, T, G, SY = nc.vector, nc.scalar, nc.tensor, nc.gpsimd, nc.sync
    skind = "ExternalOutput" if DEBUG_SCRATCH else "Internal"

    x_d = nc.dram_tensor("x", [L, D], F32, kind="ExternalInput").ap()
    w_dkv = nc.dram_tensor("wdkv", [D, DWN], F32, kind="ExternalInput").ap()
    w_uk = nc.dram_tensor("wuk", [DWN, UP], F32, kind="ExternalInput").ap()
    w_uv = nc.dram_tensor("wuv", [DWN, UP], F32, kind="ExternalInput").ap()
    w_dq = nc.dram_tensor("wdq", [D, DWN], F32, kind="ExternalInput").ap()
    w_uq = nc.dram_tensor("wuq", [DWN, UP], F32, kind="ExternalInput").ap()
    w_qr = nc.dram_tensor("wqr", [DWN, DWN], F32, kind="ExternalInput").ap()
    w_kr = nc.dram_tensor("wkr", [D, RD], F32, kind="ExternalInput").ap()
    w_fc = nc.dram_tensor("wfc", [UP, D], F32, kind="ExternalInput").ap()
    w_gu = nc.dram_tensor("wgu", [OGU, D], F32, kind="ExternalInput").ap()
    w_dn = nc.dram_tensor("wdn", [D, IF], F32, kind="ExternalInput").ap()
    b_d = {}
    for nm, szn in [("bdkv", DWN), ("buk", UP), ("buv", UP), ("bdq", DWN),
                    ("buq", UP), ("bqr", DWN), ("bkr", RD), ("bfc", D)]:
        b_d[nm] = nc.dram_tensor(nm, [1, szn], F32, kind="ExternalInput").ap()
    ident_d = nc.dram_tensor("ident", [128, 128], BF16, kind="ExternalInput").ap()
    perm_d = nc.dram_tensor("perm", [128, 128], BF16, kind="ExternalInput").ap()
    tabf_d = nc.dram_tensor("tabf", [128, 2 * L], F32, kind="ExternalInput").ap()
    out_d = nc.dram_tensor("out", [L, D], F32, kind="ExternalOutput").ap()
    w1t_s = nc.dram_tensor("w1t_s", [NOG1, 128, 16 * 512], BF16, kind="Internal").ap()
    w2t_s = nc.dram_tensor("w2t_s", [NOC2, 128, NIC2 * 128], BF16, kind="Internal").ap()
    o_s = nc.dram_tensor("o_s", [UP, L], BF16, kind=skind).ap()
    x2_s = nc.dram_tensor("x2_s", [L, D], BF16, kind=skind).ap()
    if DEBUG_SCRATCH:
        dbg_ckv = nc.dram_tensor("dbg_ckv", [128, 4 * L], BF16, kind="ExternalOutput").ap()
        dbg_cq = nc.dram_tensor("dbg_cq", [128, 4 * L], BF16, kind="ExternalOutput").ap()
        dbg_kr = nc.dram_tensor("dbg_kr", [32, L], BF16, kind="ExternalOutput").ap()
        dbg_lnT = nc.dram_tensor("dbg_lnT", [128, 16 * 512], BF16, kind="ExternalOutput").ap()
        dbg_ln1 = nc.dram_tensor("dbg_ln1", [128, D], BF16, kind="ExternalOutput").ap()
        dbg_xq = nc.dram_tensor("dbg_xq", [128, 64 * 128], BF16, kind="ExternalOutput").ap()
        dbg_h = nc.dram_tensor("dbg_h", [128, 4 * IF], BF16, kind="ExternalOutput").ap()
        dbg_hqT = nc.dram_tensor("dbg_hqT", [128, NIC2 * 512], BF16, kind="ExternalOutput").ap()
        dbg_y2 = nc.dram_tensor("dbg_y2", [128, 4 * D], BF16, kind="ExternalOutput").ap()
        dbg_w1a = nc.dram_tensor("dbg_w1a", [128, 16 * 512], BF16, kind="ExternalOutput").ap()
        dbg_w1b = nc.dram_tensor("dbg_w1b", [128, 16 * 512], BF16, kind="ExternalOutput").ap()

    stk = ExitStack()
    with stk:
        def sb(name, shape, dt):
            return stk.enter_context(nc.sbuf_tensor(name, shape, dt))
        def sem(name):
            return stk.enter_context(nc.semaphore(name))

        ident = sb("identt", [128, 128], BF16)
        perm = sb("permt", [128, 128], BF16)
        ones = sb("onest", [1, 512], BF16)
        stats = sb("statst", [128, 16], F32)
        epsb = sb("epsbt", [128, 1], F32)
        magb = sb("magbt", [128, 1], F32)
        stA = ExitStack()
        def sbA(name, shape, dt):
            return stA.enter_context(nc.sbuf_tensor(name, shape, dt))
        c_kv = sbA("c_kv", [128, 4 * L], BF16)
        c_q = sbA("c_qq", [128, 4 * L], BF16)
        k_r = sbA("k_rfm", [32, L], BF16)
        tabf = sbA("tabft", [128, 2 * L], F32)
        rt1 = sbA("rt1", [128, 512], F32)
        rt2 = sbA("rt2", [128, 512], F32)

        cn = {nm: C(sem(nm)) for nm in
              ["ld", "gld", "pe", "act", "dve", "gp",
               "qk", "xp", "avd", "rcp", "bcm", "fin", "odm", "out", "wst"]}
        vS = SelfW(sem("vself"), V)
        sS = SelfW(sem("sself"), S)

        def ln_stats(x_ap, width, dump_ap):
            i = V.reduce_sum(stats[:, 0:1], x_ap, AX.X)
            cn["dve"].inc(i)
            cn["dve"].wait(V)
            i = S.activation(dump_ap, x_ap, AF.Square, accum_out=stats[:, 1:2])
            cn["act"].inc(i)
            vS(V.tensor_scalar(stats[:, 2:3], stats[:, 0:1], -1.0 / width, None,
                               ALU.mult))
            vS(V.tensor_mul(stats[:, 3:4], stats[:, 2:3], stats[:, 2:3]))
            cn["act"].wait(V)
            i = V.scalar_tensor_tensor(stats[:, 4:5], stats[:, 1:2], 1.0 / width,
                                       stats[:, 3:4], ALU.mult, ALU.subtract)
            cn["dve"].inc(i)
            cn["dve"].wait(S)
            sS(S.activation(stats[:, 5:6], stats[:, 4:5], AF.Ln, bias=epsb[:]))
            i = S.activation(stats[:, 6:7], stats[:, 5:6], AF.Exp, scale=-0.5)
            cn["act"].inc(i)
            cn["act"].wait(V)

        def ln_apply(x_ap, out_ap):
            i = V.tensor_scalar(out_ap, x_ap, stats[:, 2:3], stats[:, 6:7],
                                ALU.add, ALU.mult)
            cn["dve"].inc(i)

        def quant_scales(src_ap, cb=7):
            vS(V.tensor_reduce(stats[:, cb:cb + 1], src_ap, AX.X, ALU.max,
                               apply_absolute_value=True))
            vS(V.tensor_scalar(stats[:, cb + 1:cb + 2], stats[:, cb:cb + 1],
                               1.0 / 127.0, 1e-8, ALU.mult, ALU.max))
            i = V.reciprocal(stats[:, cb + 2:cb + 3], stats[:, cb + 1:cb + 2])
            vS(i)
            cn["dve"].inc(V.engine_nop())

        def quant_apply(src_ap, scr_ap, dst_ap, cb=7):
            cn["dve"].wait(S)
            i = S.activation(scr_ap, src_ap, AF.Identity,
                             bias=magb[:], scale=stats[:, cb + 2:cb + 3])
            cn["act"].inc(i)
            cn["act"].wait(V)
            i = V.tensor_scalar(dst_ap, scr_ap, -MAGIC, stats[:, cb + 1:cb + 2],
                                ALU.add, ALU.mult)
            cn["dve"].inc(i)

        def rope7(pre_ap, rot_ap, out_ap, rows, col0, width):
            t1 = rt1[0:rows, 0:width]
            t2 = rt2[0:rows, 0:width]
            sl = (slice(0, rows), slice(col0, col0 + width))
            cl = (slice(0, rows), slice(L + col0, L + col0 + width))
            V.tensor_mul(t1, pre_ap, tabf[sl])
            vS(V.tensor_mul(t2, rot_ap, tabf[cl]))
            i = V.tensor_add(out_ap, t1, t2)
            cn["dve"].inc(i)

        st1 = ExitStack()
        def sb1(name, shape, dt):
            return st1.enter_context(nc.sbuf_tensor(name, shape, dt))
        wdkv = sb1("swdkv", [128, 16 * 512], BF16)
        wdq = sb1("swdq", [128, 16 * 512], BF16)
        wkr = sb1("swkr", [128, 16 * 32], BF16)
        bdkv = sb1("sbdkv", [1, DWN], BF16)
        bdq = sb1("sbdq", [1, DWN], BF16)
        bkr = sb1("sbkr", [1, RD], BF16)
        xt = sb1("xt", [128, D], F32)
        ln1 = sb1("ln1", [128, D], BF16)
        lnT = sb1("lnT", [128, 16 * 512], BF16)
        prb = sb1("prb", [32, 512], BF16)
        rob = sb1("rob", [32, 512], BF16)
        ps_tA = st1.enter_context(nc.psum_tensor("ps_tA", [128, 128], BF16))
        ps_tB = st1.enter_context(nc.psum_tensor("ps_tB", [128, 128], BF16))
        ps_pA = st1.enter_context(nc.psum_tensor("ps_pA", [128, 512], F32))
        ps_pB = st1.enter_context(nc.psum_tensor("ps_pB", [128, 512], F32))
        ps_kr = st1.enter_context(nc.psum_tensor("ps_kr", [32, 512], F32))

        def emit_ph1():
            cn["ld"].inc(SY.dma_start(out=ident[:], in_=ident_d), 16)
            cn["ld"].inc(SY.dma_start(out=perm[:], in_=perm_d), 16)
            cn["ld"].inc(SY.dma_start(out=tabf[:], in_=tabf_d), 16)
            cn["gld"].inc(G.dma_start(
                out=wdkv[:].rearrange("p (c n) -> p c n", c=16),
                in_=w_dkv.rearrange("(c p) n -> p c n", p=128)), 16)
            cn["gld"].inc(G.dma_start(
                out=wdq[:].rearrange("p (c n) -> p c n", c=16),
                in_=w_dq.rearrange("(c p) n -> p c n", p=128)), 16)
            cn["gld"].inc(G.dma_start(
                out=wkr[:].rearrange("p (c n) -> p c n", c=16),
                in_=w_kr.rearrange("(c p) n -> p c n", p=128)), 16)
            cn["gld"].inc(G.dma_start(out=bdkv[:], in_=b_d["bdkv"]), 16)
            cn["gld"].inc(G.dma_start(out=bdq[:], in_=b_d["bdq"]), 16)
            cn["gld"].inc(G.dma_start(out=bkr[:], in_=b_d["bkr"]), 16)
            V.memset(ones[:], 1.0)
            V.memset(epsb[:], 1e-5)
            V.memset(magb[:], MAGIC)
            cn["dve"].inc(V.engine_nop())
            for eng in (T, V, S):
                cn["ld"].wait(eng)
                cn["gld"].wait(eng)
            cn["dve"].wait(T)

            ev = []
            pj = []
            for st_i in range(NST):
                col0 = st_i * 512
                for ts in range(4):
                    cn["pe"].wait(G)
                    cn["dve"].wait(G)
                    cn["act"].wait(G)
                    r0 = (st_i * 4 + ts) * 128
                    cn["gld"].inc(G.dma_start(out=xt[:], in_=x_d[r0:r0 + 128, :]), 16)
                    cn["gld"].wait(V)
                    cn["gld"].wait(S)
                    cn["pe"].wait(S)
                    cn["pe"].wait(V)
                    ln_stats(xt[:], D, ln1[:])
                    ln_apply(xt[:], ln1[:])
                    cn["dve"].wait(T)
                    for c in range(16):
                        pst = ps_tA if c % 2 == 0 else ps_tB
                        if len(ev) >= 2:
                            T.wait_ge(cn["act"].sem, ev[-2])
                        i = T.transpose(pst[:], ln1[:, c * 128:(c + 1) * 128], ident[:])
                        cn["pe"].inc(i)
                        cn["pe"].wait(S)
                        i = S.copy(lnT[:, c * 512 + ts * 128:c * 512 + ts * 128 + 128],
                                   pst[:])
                        cn["act"].inc(i)
                        ev.append(cn["act"].n)
                cn["act"].wait(T)
                for (wt, bt, dst) in ((wdkv, bdkv, c_kv), (wdq, bdq, c_q)):
                    for m in range(4):
                        psp = ps_pA if m % 2 == 0 else ps_pB
                        if len(pj) >= 2:
                            T.wait_ge(cn["act"].sem, pj[-2])
                        for c in range(16):
                            T.matmul(psp[:],
                                     wt[:, c * 512 + m * 128:c * 512 + m * 128 + 128],
                                     lnT[:, c * 512:(c + 1) * 512],
                                     start=(c == 0), stop=False)
                        i = T.matmul(psp[:], bt[0:1, m * 128:(m + 1) * 128],
                                     ones[0:1, :], start=False, stop=True)
                        cn["pe"].inc(i)
                        cn["pe"].wait(S)
                        i = S.copy(dst[:, m * L + col0:m * L + col0 + 512], psp[:])
                        cn["act"].inc(i)
                        pj.append(cn["act"].n)
                cn["act"].wait(T)
                cn["dve"].wait(T)
                for c in range(16):
                    T.matmul(ps_kr[:], wkr[:, c * 32:(c + 1) * 32],
                             lnT[:, c * 512:(c + 1) * 512],
                             start=(c == 0), stop=False)
                i = T.matmul(ps_kr[:], bkr[0:1, :], ones[0:1, :],
                             start=False, stop=True)
                cn["pe"].inc(i)
                cn["pe"].wait(S)
                cn["dve"].wait(S)
                i = S.copy(prb[:], ps_kr[:])
                cn["act"].inc(i)
                cn["act"].wait(T)
                i = T.matmul(ps_kr[:], perm[0:32, 0:32], prb[:], start=True, stop=True)
                cn["pe"].inc(i)
                cn["pe"].wait(S)
                i = S.copy(rob[:], ps_kr[:])
                cn["act"].inc(i)
                cn["act"].wait(V)
                rope7(prb[:], rob[:], k_r[:, col0:col0 + 512], 32, col0, 512)

        with nc.named_scope("ph1"), nc.Block("ph1") as blk:
            @blk.sync
            def _(sy):
                emit_ph1()
                if DEBUG_SCRATCH:
                    cn["dve"].wait(SY)
                    cn["act"].wait(SY)
                    cn["out"].inc(SY.dma_start(out=dbg_ckv, in_=c_kv[:]), 16)
                    cn["out"].inc(SY.dma_start(out=dbg_cq, in_=c_q[:]), 16)
                    cn["out"].inc(SY.dma_start(out=dbg_kr, in_=k_r[:]), 16)
                    cn["out"].inc(SY.dma_start(out=dbg_lnT, in_=lnT[:]), 16)
                    cn["out"].inc(SY.dma_start(out=dbg_ln1, in_=ln1[:]), 16)
                    SY.wait_ge(cn["out"].sem, cn["out"].n)
        st1.close()

        if DEBUG_PHASE >= 2:
            st2 = ExitStack()
            def sb2(name, shape, dt):
                return st2.enter_context(nc.sbuf_tensor(name, shape, dt))
            ps_qA = st2.enter_context(nc.psum_tensor("ps_qA", [128, 128], BF16))
            ps_qB = st2.enter_context(nc.psum_tensor("ps_qB", [128, 128], BF16))

            def prep_stream(w_src, n_chunks, width, n_t, wl, wq, stg, stg_cols,
                            scratch_out, store_every):
                ldm, qm, am, tm = [], [], [], []
                stm = []
                for k in range(min(2, n_chunks)):
                    i = SY.dma_start(out=wl[k % 2][:],
                                     in_=w_src[k * 128:(k + 1) * 128, :])
                    cn["ld"].inc(i, 16)
                    ldm.append(cn["ld"].n)
                for k in range(n_chunks):
                    par = k % 2
                    cb = 7 if par == 0 else 10
                    V.wait_ge(cn["ld"].sem, ldm[k])
                    S.wait_ge(cn["ld"].sem, ldm[k])
                    quant_scales(wl[par][:], cb)
                    qm.append(cn["dve"].n)
                    cn["dve"].wait(S)
                    i = S.activation(wl[par][:], wl[par][:], AF.Identity,
                                     bias=magb[:], scale=stats[:, cb + 2:cb + 3])
                    cn["act"].inc(i)
                    am.append(cn["act"].n)
                    cn["act"].wait(V)
                    if k >= 2:
                        V.wait_ge(cn["pe"].sem, tm[k - 2])
                    i = V.tensor_scalar(wq[par][:], wl[par][:], -MAGIC,
                                        stats[:, cb + 1:cb + 2],
                                        ALU.add, ALU.mult)
                    cn["dve"].inc(i)
                    qm[-1] = cn["dve"].n
                    if k + 2 < n_chunks:
                        SY.wait_ge(cn["act"].sem, am[k])
                        SY.wait_ge(cn["dve"].sem, qm[k])
                        i = SY.dma_start(out=wl[par][:],
                                         in_=w_src[(k + 2) * 128:(k + 3) * 128, :])
                        cn["ld"].inc(i, 16)
                        ldm.append(cn["ld"].n)
                    T.wait_ge(cn["dve"].sem, qm[k])
                    if k % store_every == 0 and stm:
                        S.wait_ge(cn["gp"].sem, stm[-1])
                    tev = []
                    for c in range(n_t):
                        psq = ps_qA if c % 2 == 0 else ps_qB
                        if len(tev) >= 2:
                            T.wait_ge(cn["act"].sem, tev[-2])
                        elif k > 0:
                            T.wait_ge(cn["act"].sem, cn["act"].n - 1)
                        i = T.transpose(psq[:], wq[par][:, c * 128:(c + 1) * 128],
                                        ident[:])
                        cn["pe"].inc(i)
                        cn["pe"].wait(S)
                        i = S.copy(stg[:, stg_cols(k, c):stg_cols(k, c) + 128],
                                   psq[:])
                        cn["act"].inc(i)
                        tev.append(cn["act"].n)
                    tm.append(cn["pe"].n)
                    if k % store_every == store_every - 1:
                        cn["act"].wait(G)
                        i = G.dma_start(out=scratch_out[k // store_every],
                                        in_=stg[:])
                        cn["gp"].inc(i, 16)
                        stm.append(cn["gp"].n)
                cn["gp"].wait(SY)
                cn["gp"].wait(G)

            def emit_prep():
                st2a = ExitStack()
                w1lA = st2a.enter_context(nc.sbuf_tensor("w1lA", [128, D], F32))
                w1lB = st2a.enter_context(nc.sbuf_tensor("w1lB", [128, D], F32))
                w1qA = st2a.enter_context(nc.sbuf_tensor("w1qA", [128, D], BF16))
                w1qB = st2a.enter_context(nc.sbuf_tensor("w1qB", [128, D], BF16))
                stg1 = st2a.enter_context(nc.sbuf_tensor("stg1", [128, 16 * 512], BF16))
                prep_stream(w_gu, 88, D, 16, [w1lA, w1lB], [w1qA, w1qB], stg1[:],
                            lambda k, c: c * 512 + (k % 4) * 128, w1t_s, 4)
                st2a.close()
                st2b = ExitStack()
                w2lA = st2b.enter_context(nc.sbuf_tensor("w2lA", [128, IF], F32))
                w2lB = st2b.enter_context(nc.sbuf_tensor("w2lB", [128, IF], F32))
                w2qA = st2b.enter_context(nc.sbuf_tensor("w2qA", [128, IF], BF16))
                w2qB = st2b.enter_context(nc.sbuf_tensor("w2qB", [128, IF], BF16))
                stg2 = st2b.enter_context(nc.sbuf_tensor("stg2", [128, NIC2 * 128], BF16))
                prep_stream(w_dn, 16, IF, NIC2, [w2lA, w2lB], [w2qA, w2qB], stg2[:],
                            lambda k, c: c * 128, w2t_s, 1)
                st2b.close()

            with nc.named_scope("prep"), nc.Block("prep") as blk:
                @blk.sync
                def _(sy):
                    emit_prep()
            st2.close()

        if DEBUG_PHASE >= 3:
            st3 = ExitStack()
            def sb3(name, shape, dt):
                return st3.enter_context(nc.sbuf_tensor(name, shape, dt))
            wuk = sb3("swuk", [128, 4 * UP], BF16)
            wuv = sb3("swuv", [128, 4 * UP], BF16)
            wuq = sb3("swuq", [128, 4 * UP], BF16)
            wqr = sb3("swqr", [128, 4 * DWN], BF16)
            buk = sb3("sbuk", [1, UP], BF16)
            buv = sb3("sbuv", [1, UP], BF16)
            buq = sb3("sbuq", [1, UP], BF16)
            bqr = sb3("sbqr", [1, DWN], BF16)
            K_h = sb3("K_h", [96, L], BF16)
            V_h = sb3("V_h", [128, NKV * 65], BF16)
            q_h = sb3("q_h", [96, L], BF16)
            q_r4 = sb3("q_r4", [128, L], BF16)
            pre4 = sb3("pre4", [128, L], BF16)
            ro4A = sb3("ro4A", [128, 512], BF16)
            ro4B = sb3("ro4B", [128, 512], BF16)
            ptA = sb3("ptA", [128, 1024], BF16)
            ptB = sb3("ptB", [128, 1024], BF16)
            osbA = sb3("osbA", [64, 512], BF16)
            osbB = sb3("osbB", [64, 512], BF16)
            linv = sb3("linv", [1, 512], F32)
            onesf = sb3("onesf", [1, 64], F32)
            lbc = sb3("lbc", [64, 512], F32)
            ps_sA = st3.enter_context(nc.psum_tensor("ps_sA", [128, 1024], F32))
            ps_sB = st3.enter_context(nc.psum_tensor("ps_sB", [128, 1024], F32))
            ps_oA = st3.enter_context(nc.psum_tensor("ps_oA", [128, 512], F32))
            ps_oB = st3.enter_context(nc.psum_tensor("ps_oB", [128, 512], F32))
            ps_bA = st3.enter_context(nc.psum_tensor("ps_bA", [128, 512], F32))
            ps_bB = st3.enter_context(nc.psum_tensor("ps_bB", [128, 512], F32))
            ps_lb = None
            ps_s = [ps_sA, ps_sB]
            ps_o = [ps_oA, ps_oB]
            pt = [ptA, ptB]
            osb = [osbA, osbB]
            ps_bd = [ps_bA, ps_bB]

            def emit_ph2():
                cn["gld"].inc(G.dma_start(
                    out=wuk[:].rearrange("p (c n) -> p c n", c=4),
                    in_=w_uk.rearrange("(c p) n -> p c n", p=128)), 16)
                cn["gld"].inc(G.dma_start(
                    out=wuv[:].rearrange("p (c n) -> p c n", c=4),
                    in_=w_uv.rearrange("(c p) n -> p c n", p=128)), 16)
                cn["gld"].inc(G.dma_start(
                    out=wuq[:].rearrange("p (c n) -> p c n", c=4),
                    in_=w_uq.rearrange("(c p) n -> p c n", p=128)), 16)
                cn["gld"].inc(G.dma_start(
                    out=wqr[:].rearrange("p (c n) -> p c n", c=4),
                    in_=w_qr.rearrange("(c p) n -> p c n", p=128)), 16)
                cn["gld"].inc(G.dma_start(out=buk[:], in_=b_d["buk"]), 16)
                cn["gld"].inc(G.dma_start(out=buv[:], in_=b_d["buv"]), 16)
                cn["gld"].inc(G.dma_start(out=buq[:], in_=b_d["buq"]), 16)
                cn["gld"].inc(G.dma_start(out=bqr[:], in_=b_d["bqr"]), 16)
                for eng in (T, V, S):
                    cn["gld"].wait(eng)
                V.memset(onesf[:], 1.0)
                for t in range(NKV):
                    V.memset(V_h[:, t * 65 + 64:t * 65 + 65], 1.0)
                cn["dve"].inc(V.engine_nop())
                cn["dve"].wait(T)

                bmk = []

                def build_fm64(src, wt, bt, h0, dst, row0, scale_imm):
                    for n in range(8):
                        psp = ps_bd[n % 2]
                        if len(bmk) >= 2:
                            T.wait_ge(cn["dve"].sem, bmk[-2])
                        for c in range(4):
                            T.matmul(psp[0:64, :],
                                     wt[:, c * UP + h0:c * UP + h0 + 64],
                                     src[:, c * L + n * 512:c * L + n * 512 + 512],
                                     start=(c == 0), stop=False)
                        i = T.matmul(psp[0:64, :], bt[0:1, h0:h0 + 64],
                                     ones[0:1, :], start=False, stop=True)
                        cn["pe"].inc(i)
                        cn["pe"].wait(V)
                        i = V.tensor_scalar(dst[row0:row0 + 64, n * 512:(n + 1) * 512],
                                            psp[0:64, :], scale_imm, None, ALU.mult)
                        cn["dve"].inc(i)
                        bmk.append(cn["dve"].n)

                def emit_fin_pe(I):
                    T.wait_ge(cn["fin"].sem, I)
                    T.wait_ge(cn["rcp"].sem, I + 1)
                    cn["dve"].wait(T)
                    i = T.matmul(ps_bA[0:64, :], onesf[0:1, :], linv[0:1, :],
                                 start=True, stop=True)
                    cn["bcm"].inc(i)

                def emit_fin_rest(I):
                    h_i, qs_i = I // NST, I % NST
                    V.wait_ge(cn["bcm"].sem, I + 1)
                    vS(V.tensor_copy(lbc[:], ps_bA[0:64, :]))
                    if I >= 2:
                        V.wait_ge(cn["odm"].sem, 16 * (I - 1))
                    i = V.tensor_mul(osb[I % 2][:], ps_o[I % 2][0:64, :], lbc[:])
                    cn["fin"].inc(i)
                    G.wait_ge(cn["fin"].sem, I + 1)
                    i = G.dma_start(out=o_s[h_i * 64:(h_i + 1) * 64,
                                            qs_i * 512:(qs_i + 1) * 512],
                                    in_=osb[I % 2][:])
                    cn["odm"].inc(i, 16)

                gidx = 0
                for h in range(NH):
                    cn["fin"].wait(T)
                    build_fm64(c_kv, wuk, buk, h * 64, K_h, 0, 1.0)
                    V.wait_ge(cn["qk"].sem, h * 256) if h > 0 else None
                    cn["dve"].inc(V.tensor_copy(K_h[64:96, :], k_r[:]))
                    for t in range(NKV):
                        psp = ps_bd[t % 2]
                        if len(bmk) >= 2:
                            T.wait_ge(cn["dve"].sem, bmk[-2])
                        for c in range(4):
                            T.matmul(psp[:, 0:64],
                                     c_kv[:, c * L + t * 128:c * L + t * 128 + 128],
                                     wuv[:, c * UP + h * 64:c * UP + h * 64 + 64],
                                     start=(c == 0), stop=False)
                        i = T.matmul(psp[:, 0:64], ones[0:1, 0:128],
                                     buv[0:1, h * 64:h * 64 + 64],
                                     start=False, stop=True)
                        cn["pe"].inc(i)
                        cn["pe"].wait(V)
                        i = V.tensor_copy(V_h[:, t * 65:t * 65 + 64], psp[:, 0:64])
                        cn["dve"].inc(i)
                        bmk.append(cn["dve"].n)
                    if h % 4 == 0:
                        g4 = h // 4
                        cn["gld"].wait(V)
                        cn["gld"].wait(S)
                        for n in range(8):
                            psp = ps_bd[n % 2]
                            if len(bmk) >= 2:
                                T.wait_ge(cn["act"].sem, bmk[-2])
                            for c in range(4):
                                T.matmul(psp[:],
                                         wqr[:, c * DWN + g4 * 128:
                                             c * DWN + g4 * 128 + 128],
                                         c_q[:, c * L + n * 512:c * L + n * 512 + 512],
                                         start=(c == 0), stop=False)
                            i = T.matmul(psp[:], bqr[0:1, g4 * 128:g4 * 128 + 128],
                                         ones[0:1, :], start=False, stop=True)
                            cn["pe"].inc(i)
                            cn["pe"].wait(V)
                            i = V.tensor_scalar(pre4[:, n * 512:(n + 1) * 512],
                                                psp[:, 0:512], SCALE, None, ALU.mult)
                            cn["dve"].inc(i)
                            bmk.append(cn["dve"].n)
                        for n in range(8):
                            psp = ps_bd[n % 2]
                            rb = ro4A if n % 2 == 0 else ro4B
                            if len(bmk) >= 2:
                                T.wait_ge(cn["dve"].sem, bmk[-2])
                            i = T.matmul(psp[:], perm[:],
                                         pre4[:, n * 512:(n + 1) * 512],
                                         start=True, stop=True)
                            cn["pe"].inc(i)
                            cn["pe"].wait(V)
                            i = V.tensor_copy(rb[:], psp[:, 0:512])
                            cn["dve"].inc(i)
                            bmk.append(cn["dve"].n)
                            rope7(pre4[:, n * 512:(n + 1) * 512], rb[:],
                                  q_r4[:, n * 512:(n + 1) * 512], 128, n * 512, 512)
                    build_fm64(c_q, wuq, buq, h * 64, q_h, 0, SCALE)
                    cn["dve"].wait(G)
                    cn["act"].wait(G)
                    i = G.dma_start(out=q_h[64:96, :],
                                    in_=q_r4[(h % 4) * 32:(h % 4) * 32 + 32, :])
                    cn["gld"].inc(i, 16)
                    cn["gld"].wait(T)
                    cn["dve"].wait(T)
                    cn["fin"].wait(T)

                    for qs in range(NST):
                        I = gidx
                        for p in range(16):
                            pi = I * 16 + p
                            if pi >= 2:
                                T.wait_ge(cn["xp"].sem, pi - 1)
                            T.matmul(ps_s[p % 2][:, 0:512],
                                     K_h[:, (2 * p) * 128:(2 * p + 1) * 128],
                                     q_h[:, qs * 512:(qs + 1) * 512],
                                     start=True, stop=True)
                            i = T.matmul(ps_s[p % 2][:, 512:1024],
                                         K_h[:, (2 * p + 1) * 128:(2 * p + 2) * 128],
                                         q_h[:, qs * 512:(qs + 1) * 512],
                                         start=True, stop=True)
                            cn["qk"].inc(i, 2)
                            S.wait_ge(cn["qk"].sem, I * 32 + 2 * p + 2)
                            i = S.activation(pt[p % 2][:], ps_s[p % 2][:], AF.Exp)
                            cn["xp"].inc(i)
                            if p == 1 and I >= 2:
                                T.wait_ge(cn["fin"].sem, I - 1)
                            if p >= 1:
                                T.wait_ge(cn["xp"].sem, pi)
                                cc = 2 * p - 2
                                T.matmul(ps_o[I % 2][0:65, :],
                                         V_h[:, cc * 65:cc * 65 + 65],
                                         pt[(p - 1) % 2][:, 0:512],
                                         start=(cc == 0), stop=False)
                                T.matmul(ps_o[I % 2][0:65, :],
                                         V_h[:, (cc + 1) * 65:(cc + 1) * 65 + 65],
                                         pt[(p - 1) % 2][:, 512:1024],
                                         start=False, stop=False)
                            if p == 5 and I >= 1:
                                emit_fin_pe(I - 1)
                            if p == 7 and I >= 1:
                                emit_fin_rest(I - 1)
                        T.wait_ge(cn["xp"].sem, I * 16 + 16)
                        T.matmul(ps_o[I % 2][0:65, :],
                                 V_h[:, 30 * 65:30 * 65 + 65],
                                 pt[1][:, 0:512], start=False, stop=False)
                        i = T.matmul(ps_o[I % 2][0:65, :],
                                     V_h[:, 31 * 65:31 * 65 + 65],
                                     pt[1][:, 512:1024], start=False, stop=True)
                        cn["avd"].inc(i)
                        V.wait_ge(cn["avd"].sem, I + 1)
                        if I >= 1:
                            V.wait_ge(cn["bcm"].sem, I)
                        i = V.reciprocal(linv[:], ps_o[I % 2][64:65, :])
                        cn["rcp"].inc(i)
                        gidx += 1
                emit_fin_pe(gidx - 1)
                emit_fin_rest(gidx - 1)

            with nc.named_scope("ph2"), nc.Block("ph2") as blk:
                @blk.sync
                def _(sy):
                    emit_ph2()
            st3.close()

        stA.close()

        if DEBUG_PHASE >= 4:
            st4 = ExitStack()
            def sb4(name, shape, dt):
                return st4.enter_context(nc.sbuf_tensor(name, shape, dt))
            wfc = sb4("swfc", [128, 8 * D], BF16)
            bfc = sb4("sbfc", [1, D], BF16)
            ot = sb4("ot", [128, 8 * 512], BF16)
            xt3 = sb4("xt3", [128, D], F32)
            fcs = sb4("fcs", [128, D], F32)
            lns = sb4("lns", [128, D], F32)
            x2sb = sb4("x2sb", [128, D], F32)
            dmp3 = sb4("dmp3", [128, D], BF16)
            ps_fA = st4.enter_context(nc.psum_tensor("ps_fA", [128, 2048], F32))
            ps_fB = st4.enter_context(nc.psum_tensor("ps_fB", [128, 2048], F32))

            def emit_ph3():
                cn["gld"].inc(G.dma_start(
                    out=wfc[:].rearrange("p (c n) -> p c n", c=8),
                    in_=w_fc.rearrange("(c p) n -> p c n", p=128)), 16)
                cn["gld"].inc(G.dma_start(out=bfc[:], in_=b_d["bfc"]), 16)
                for eng in (T, V, S):
                    cn["gld"].wait(eng)
                fcev = []
                for st_i in range(NST):
                    cn["pe"].wait(SY)
                    for c in range(8):
                        i = SY.dma_start(out=ot[:, c * 512:(c + 1) * 512],
                                         in_=o_s[c * 128:(c + 1) * 128,
                                                 st_i * 512:(st_i + 1) * 512])
                        cn["ld"].inc(i, 16)
                    cn["ld"].wait(T)
                    for ts in range(4):
                        psf = ps_fA if ts % 2 == 0 else ps_fB
                        r0 = (st_i * 4 + ts) * 128
                        cn["dve"].wait(G)
                        cn["gld"].inc(G.dma_start(out=xt3[:],
                                                  in_=x_d[r0:r0 + 128, :]), 16)
                        if len(fcev) >= 2:
                            T.wait_ge(cn["act"].sem, fcev[-2])
                        for og in range(4):
                            for c in range(8):
                                T.matmul(psf[:, og * 512:(og + 1) * 512],
                                         ot[:, c * 512 + ts * 128:
                                            c * 512 + ts * 128 + 128],
                                         wfc[:, c * D + og * 512:
                                             c * D + og * 512 + 512],
                                         start=(c == 0), stop=False)
                            i = T.matmul(psf[:, og * 512:(og + 1) * 512],
                                         ones[0:1, 0:128],
                                         bfc[0:1, og * 512:(og + 1) * 512],
                                         start=False, stop=True)
                            cn["pe"].inc(i)
                        cn["pe"].wait(S)
                        cn["dve"].wait(S)
                        i = S.copy(fcs[:], psf[:])
                        cn["act"].inc(i)
                        fcev.append(cn["act"].n)
                        cn["act"].wait(V)
                        ln_stats(fcs[:], D, dmp3[:])
                        ln_apply(fcs[:], lns[:])
                        cn["dve"].wait(V)
                        cn["gld"].wait(V)
                        cn["gp"].wait(V)
                        vS(V.tensor_add(x2sb[:], xt3[:], lns[:]))
                        cn["dve"].inc(V.engine_nop())
                        cn["dve"].wait(G)
                        i = G.dma_start(out=x2_s[r0:r0 + 128, :], in_=x2sb[:])
                        cn["gp"].inc(i, 16)

            with nc.named_scope("ph3"), nc.Block("ph3") as blk:
                @blk.sync
                def _(sy):
                    emit_ph3()
            st4.close()

        if DEBUG_PHASE >= 5:
            st5 = ExitStack()
            def sb5(name, shape, dt):
                return st5.enter_context(nc.sbuf_tensor(name, shape, dt))
            x2t = sb5("x2t", [128, D], BF16)
            scr4 = sb5("scr4", [128, D], F32)
            xq = sb5("xq", [128, D], BF16)
            xqT = sb5("xqT", [128, 64 * 128], BF16)
            w1sA = sb5("w1sA", [128, 16 * 512], BF16)
            w1sB = sb5("w1sB", [128, 16 * 512], BF16)
            gsb = sb5("gsb", [128, 2048], BF16)
            hbuf = sb5("hbuf", [128, 4 * IF], BF16)
            hqT = sb5("hqT", [128, NIC2 * 512], BF16)
            w2sA = sb5("w2sA", [128, NIC2 * 128], BF16)
            w2sB = sb5("w2sB", [128, NIC2 * 128], BF16)
            y2f = sb5("y2f", [128, 512], BF16)
            y2tm = sb5("y2tm", [128, 4 * D], BF16)
            qscA = sb5("qscA", [128, 512], F32)
            qscB = sb5("qscB", [128, 512], F32)
            ps_a = st5.enter_context(nc.psum_tensor("ps_a", [128, 2048], F32))
            ps_b = st5.enter_context(nc.psum_tensor("ps_b", [128, 2048], F32))
            psa_bf = ps_a[:].bitcast(BF16)
            psb_bf = ps_b[:].bitcast(BF16)
            w1s = [w1sA, w1sB]
            w2s = [w2sA, w2sB]

            def emit_ph4():
                stg = DEBUG_P4STAGE
                for tt in range(NST):
                    for ts in range(4):
                        r0 = (tt * 4 + ts) * 128
                        cn["dve"].wait(G)
                        cn["act"].wait(G)
                        cn["gld"].inc(G.dma_start(out=x2t[:],
                                                  in_=x2_s[r0:r0 + 128, :]), 16)
                        cn["gld"].wait(V)
                        cn["gld"].wait(S)
                        ln_stats(x2t[:], D, xq[:])
                        V.tensor_reduce(stats[:, 12:13], x2t[:], AX.X, ALU.max)
                        vS(V.tensor_reduce(stats[:, 13:14], x2t[:], AX.X, ALU.min))
                        V.tensor_scalar(stats[:, 14:15], stats[:, 12:13],
                                        stats[:, 2:3], None, ALU.add)
                        vS(V.tensor_scalar(stats[:, 15:16], stats[:, 13:14],
                                           stats[:, 2:3], -1.0, ALU.add, ALU.mult))
                        vS(V.tensor_max(stats[:, 7:8], stats[:, 14:15],
                                        stats[:, 15:16]))
                        vS(V.tensor_mul(stats[:, 7:8], stats[:, 7:8], stats[:, 6:7]))
                        vS(V.tensor_scalar(stats[:, 8:9], stats[:, 7:8], 1.0 / 127.0,
                                           1e-8, ALU.mult, ALU.max))
                        vS(V.reciprocal(stats[:, 9:10], stats[:, 8:9]))
                        vS(V.tensor_mul(stats[:, 10:11], stats[:, 6:7],
                                        stats[:, 9:10]))
                        i = V.tensor_scalar(stats[:, 11:12], stats[:, 2:3],
                                            stats[:, 10:11], MAGIC, ALU.mult, ALU.add)
                        vS(i)
                        cn["dve"].inc(V.engine_nop())
                        cn["dve"].wait(S)
                        cn["out"].wait(S)
                        i = S.activation(scr4[:], x2t[:], AF.Identity,
                                         bias=stats[:, 11:12], scale=stats[:, 10:11])
                        cn["act"].inc(i)
                        cn["act"].wait(V)
                        i = V.tensor_scalar(xq[:], scr4[:], -MAGIC, stats[:, 8:9],
                                            ALU.add, ALU.mult)
                        cn["dve"].inc(i)
                        cn["dve"].wait(T)
                        for c in range(16):
                            pv = psa_bf[:, 0:128] if c % 2 == 0 else psa_bf[:, 1024:1152]
                            if c >= 2:
                                T.wait_ge(cn["act"].sem, cn["act"].n - 1)
                            i = T.transpose(pv, xq[:, c * 128:(c + 1) * 128], ident[:])
                            cn["pe"].inc(i)
                            cn["pe"].wait(S)
                            i = S.copy(xqT[:, (c * 4 + ts) * 128:
                                           (c * 4 + ts) * 128 + 128], pv)
                            cn["act"].inc(i)
                    if stg < "B":
                        continue
                    cn["pe"].wait(SY)
                    wgm, wum = [], []
                    cn["wst"].inc(SY.dma_start(out=w1s[0][:], in_=w1t_s[0]), 16)
                    wgm.append(cn["wst"].n)
                    cn["wst"].inc(SY.dma_start(out=w1s[1][:], in_=w1t_s[11]), 16)
                    wum.append(cn["wst"].n)
                    gmk, umk, smk, mmk = [], [], [], []
                    for p in range(11):
                        T.wait_ge(cn["wst"].sem, wgm[p])
                        if p >= 1:
                            T.wait_ge(cn["act"].sem, smk[-1])
                        else:
                            cn["act"].wait(T)
                        last = None
                        for ts in range(4):
                            for c in range(16):
                                last = T.matmul(ps_a[:, ts * 512:(ts + 1) * 512],
                                                xqT[:, (c * 4 + ts) * 128:
                                                    (c * 4 + ts) * 128 + 128],
                                                w1s[0][:, c * 512:(c + 1) * 512],
                                                start=(c == 0), stop=(c == 15))
                        cn["pe"].inc(last)
                        gmk.append(cn["pe"].n)
                        if p < 10:
                            SY.wait_ge(cn["pe"].sem, gmk[-1])
                            cn["wst"].inc(SY.dma_start(out=w1s[0][:],
                                                       in_=w1t_s[p + 1]), 16)
                            wgm.append(cn["wst"].n)
                        S.wait_ge(cn["pe"].sem, gmk[-1])
                        if p >= 1:
                            S.wait_ge(cn["dve"].sem, mmk[-1])
                        i = S.activation(gsb[:], ps_a[:], AF.Silu)
                        cn["act"].inc(i)
                        smk.append(cn["act"].n)
                        T.wait_ge(cn["wst"].sem, wum[p])
                        if p >= 1:
                            T.wait_ge(cn["dve"].sem, mmk[-1])
                        last = None
                        for ts in range(4):
                            for c in range(16):
                                last = T.matmul(ps_b[:, ts * 512:(ts + 1) * 512],
                                                xqT[:, (c * 4 + ts) * 128:
                                                    (c * 4 + ts) * 128 + 128],
                                                w1s[1][:, c * 512:(c + 1) * 512],
                                                start=(c == 0), stop=(c == 15))
                        cn["pe"].inc(last)
                        umk.append(cn["pe"].n)
                        if p < 10:
                            SY.wait_ge(cn["pe"].sem, umk[-1])
                            cn["wst"].inc(SY.dma_start(out=w1s[1][:],
                                                       in_=w1t_s[12 + p]), 16)
                            wum.append(cn["wst"].n)
                        V.wait_ge(cn["pe"].sem, umk[-1])
                        V.wait_ge(cn["act"].sem, smk[-1])
                        for ts in range(4):
                            i = V.tensor_mul(
                                hbuf[:, ts * IF + p * 512:ts * IF + (p + 1) * 512],
                                gsb[:, ts * 512:(ts + 1) * 512],
                                ps_b[:, ts * 512:(ts + 1) * 512])
                            cn["dve"].inc(i)
                        mmk.append(cn["dve"].n)
                    if DEBUG_SCRATCH and tt == NST - 1:
                        cn["dve"].wait(G)
                        i = G.dma_start(out=dbg_h, in_=hbuf[:])
                        cn["gld"].inc(i, 16)
                        cn["gld"].wait(S)
                        cn["gld"].wait(V)
                        cn["gld"].wait(T)
                    if stg < "C":
                        continue
                    for ts in range(4):
                        hsl = hbuf[:, ts * IF:(ts + 1) * IF]
                        cn["dve"].wait(V)
                        cn["pe"].wait(S)
                        quant_scales(hsl)
                        cn["dve"].wait(S)
                        qmks = []
                        for piece in range(11):
                            pp = hbuf[:, ts * IF + piece * 512:
                                      ts * IF + (piece + 1) * 512]
                            qs_t = qscA if piece % 2 == 0 else qscB
                            if len(qmks) >= 2:
                                S.wait_ge(cn["dve"].sem, qmks[-2])
                            i = S.activation(qs_t[:], pp, AF.Identity,
                                             bias=magb[:], scale=stats[:, 9:10])
                            cn["act"].inc(i)
                            cn["act"].wait(V)
                            i = V.tensor_scalar(pp, qs_t[:], -MAGIC, stats[:, 8:9],
                                                ALU.add, ALU.mult)
                            cn["dve"].inc(i)
                            qmks.append(cn["dve"].n)
                        cn["dve"].wait(T)
                        for c in range(NIC2):
                            pv = (psa_bf[:, 0:128] if c % 2 == 0
                                  else psa_bf[:, 1024:1152])
                            if c >= 2:
                                T.wait_ge(cn["act"].sem, cn["act"].n - 1)
                            i = T.transpose(pv, hbuf[:, ts * IF + c * 128:
                                                     ts * IF + c * 128 + 128],
                                            ident[:])
                            cn["pe"].inc(i)
                            cn["pe"].wait(S)
                            i = S.copy(hqT[:, c * 512 + ts * 128:
                                           c * 512 + ts * 128 + 128], pv)
                            cn["act"].inc(i)
                    if stg < "E":
                        continue
                    cn["pe"].wait(SY)
                    w2m = []
                    cn["wst"].inc(SY.dma_start(out=w2s[0][:], in_=w2t_s[0]), 16)
                    w2m.append(cn["wst"].n)
                    cn["wst"].inc(SY.dma_start(out=w2s[1][:], in_=w2t_s[1]), 16)
                    w2m.append(cn["wst"].n)
                    dmk, yev, ytv = [], [], []
                    for oc in range(NOC2):
                        T.wait_ge(cn["wst"].sem, w2m[oc])
                        psy = ps_a[:, 0:512] if oc % 2 == 0 else ps_b[:, 0:512]
                        if oc == 0:
                            cn["act"].wait(T)
                        if len(yev) >= 2:
                            T.wait_ge(cn["act"].sem, yev[-2])
                        last = None
                        for c in range(NIC2):
                            last = T.matmul(psy, w2s[oc % 2][:, c * 128:(c + 1) * 128],
                                            hqT[:, c * 512:(c + 1) * 512],
                                            start=(c == 0), stop=(c == NIC2 - 1))
                        cn["pe"].inc(last)
                        dmk.append(cn["pe"].n)
                        if oc + 2 < NOC2:
                            SY.wait_ge(cn["pe"].sem, dmk[-1])
                            cn["wst"].inc(SY.dma_start(out=w2s[oc % 2][:],
                                                       in_=w2t_s[oc + 2]), 16)
                            w2m.append(cn["wst"].n)
                        S.wait_ge(cn["pe"].sem, dmk[-1])
                        if oc >= 1:
                            S.wait_ge(cn["pe"].sem, ytv[-1])
                        i = S.copy(y2f[:], psy)
                        cn["act"].inc(i)
                        yev.append(cn["act"].n)
                        T.wait_ge(cn["act"].sem, yev[-1])
                        for ts in range(4):
                            pv = (psb_bf[:, 2048:2176] if ts % 2 == 0
                                  else psb_bf[:, 3072:3200])
                            if len(ytv) >= 1 or ts >= 2:
                                cn["dve"].wait(T)
                            i = T.transpose(pv, y2f[:, ts * 128:(ts + 1) * 128],
                                            ident[:])
                            cn["pe"].inc(i)
                            cn["pe"].wait(V)
                            i = V.tensor_copy(y2tm[:, ts * D + oc * 128:
                                                  ts * D + oc * 128 + 128], pv)
                            cn["dve"].inc(i)
                        ytv.append(cn["pe"].n)
                    if stg < "F":
                        continue
                    for ts in range(4):
                        r0 = (tt * 4 + ts) * 128
                        ysl = y2tm[:, ts * D:(ts + 1) * D]
                        cn["act"].wait(G)
                        cn["dve"].wait(G)
                        cn["gld"].inc(G.dma_start(out=x2t[:],
                                                  in_=x2_s[r0:r0 + 128, :]), 16)
                        cn["out"].wait(S)
                        cn["out"].wait(V)
                        cn["dve"].wait(V)
                        cn["pe"].wait(S)
                        ln_stats(ysl, D, xq[:])
                        ln_apply(ysl, scr4[:])
                        cn["dve"].wait(V)
                        cn["gld"].wait(V)
                        vS(V.tensor_add(scr4[:], scr4[:], x2t[:]))
                        cn["dve"].inc(V.engine_nop())
                        cn["dve"].wait(SY)
                        i = SY.dma_start(out=out_d[r0:r0 + 128, :], in_=scr4[:])
                        cn["out"].inc(i, 16)
                if DEBUG_SCRATCH:
                    cn["dve"].wait(SY)
                    cn["act"].wait(SY)
                    cn["out"].inc(SY.dma_start(out=dbg_w1a, in_=w1t_s[10]), 16)
                    cn["out"].inc(SY.dma_start(out=dbg_w1b, in_=w1t_s[21]), 16)
                    cn["out"].inc(SY.dma_start(out=dbg_xq, in_=xqT[:]), 16)
                    cn["out"].inc(SY.dma_start(out=dbg_hqT, in_=hqT[:]), 16)
                    cn["out"].inc(SY.dma_start(out=dbg_y2, in_=y2tm[:]), 16)
                if cn["out"].n:
                    SY.wait_ge(cn["out"].sem, cn["out"].n)

            with nc.named_scope("ph4"), nc.Block("ph4") as blk:
                @blk.sync
                def _(sy):
                    emit_ph4()
            st5.close()
    return nc


def _host_consts():
    bf = ml_dtypes.bfloat16
    ident = np.eye(128, dtype=bf)
    p32 = np.zeros((32, 32), dtype=np.float32)
    for i in range(16):
        p32[2 * i + 1, 2 * i] = -1.0
        p32[2 * i, 2 * i + 1] = 1.0
    perm = np.zeros((128, 128), dtype=np.float32)
    for b in range(4):
        perm[b * 32:(b + 1) * 32, b * 32:(b + 1) * 32] = p32
    perm = perm.astype(bf)
    theta = np.exp(np.arange(0, RD, 2, dtype=np.float64) * (-np.log(10000.0) / RD))
    ang = np.arange(L, dtype=np.float64)[None, :] * theta[:, None]
    sinr = np.repeat(np.sin(ang), 2, axis=0)
    cosr = np.repeat(np.cos(ang), 2, axis=0)
    sin4 = np.tile(sinr, (4, 1))
    cos4 = np.tile(cosr, (4, 1))
    tab = np.concatenate([sin4, cos4], axis=1).astype(np.float32)
    return ident, perm, np.ascontiguousarray(tab)


_NC_CACHE = {}


def kernel(**inputs):
    key = (DEBUG_PHASE, DEBUG_SCRATCH, DEBUG_P4STAGE)
    if key not in _NC_CACHE:
        _NC_CACHE[key] = build_nc()
    nc = _NC_CACHE[key]
    ident, perm, tabf = _host_consts()
    x = np.ascontiguousarray(np.asarray(inputs["L0_attn_h"], dtype=np.float32))
    common = {
        "wdkv": inputs["L0_attn_W_dkv"], "wuk": inputs["L0_attn_W_uk"],
        "wuv": inputs["L0_attn_W_uv"], "wdq": inputs["L0_attn_W_dq"],
        "wuq": inputs["L0_attn_W_uq"], "wqr": inputs["L0_attn_W_qr"],
        "wkr": inputs["L0_attn_W_kr"], "wfc": inputs["L0_attn_W_fc"],
        "wgu": inputs["L0_ffn_w_gate_up"], "wdn": inputs["L0_ffn_w_down"],
        "bdkv": np.asarray(inputs["L0_attn_b_dkv"], np.float32).reshape(1, -1),
        "buk": np.asarray(inputs["L0_attn_b_uk"], np.float32).reshape(1, -1),
        "buv": np.asarray(inputs["L0_attn_b_uv"], np.float32).reshape(1, -1),
        "bdq": np.asarray(inputs["L0_attn_b_dq"], np.float32).reshape(1, -1),
        "buq": np.asarray(inputs["L0_attn_b_uq"], np.float32).reshape(1, -1),
        "bqr": np.asarray(inputs["L0_attn_b_qr"], np.float32).reshape(1, -1),
        "bkr": np.asarray(inputs["L0_attn_b_kr"], np.float32).reshape(1, -1),
        "bfc": np.asarray(inputs["L0_attn_b_fc"], np.float32).reshape(1, -1),
        "ident": ident, "perm": perm, "tabf": tabf,
    }
    common = {k: np.ascontiguousarray(np.asarray(v, dtype=np.float32))
              if np.asarray(v).dtype == np.float32 or k.startswith(("w", "b"))
              else v for k, v in common.items()}
    common["ident"], common["perm"], common["tabf"] = ident, perm, tabf
    in_maps = [dict(common, x=np.ascontiguousarray(x[i])) for i in range(8)]
    res = run_bass_kernel_spmd(nc, in_maps, core_ids=list(range(8)))
    out = np.stack([r["out"] for r in res.results], axis=0).astype(np.float32)
    if DEBUG_SCRATCH:
        kernel.debug_results = res.results
    return out
